# version 26
# baseline (speedup 1.0000x reference)
import math
import numpy as np
import ml_dtypes
import concourse.bass as bass
import concourse.mybir as mybir
from concourse.bass_utils import run_bass_kernel_spmd

F32 = mybir.dt.float32
BF16 = mybir.dt.bfloat16
AF = mybir.ActivationFunctionType
ALU = mybir.AluOpType
AX = mybir.AxisListType
BF = ml_dtypes.bfloat16

NCORES = 8
B, S, D = 16, 2048, 1024
BPC = B // NCORES
H, d = 8, 128
C = 128
NCH = S // C
NCHUNKS = BPC * NCH
FH = 4096
HG = 4


def _make_consts():
    g = 1.0 - np.exp(np.linspace(math.log(1 / 32.0), math.log(1 / 512.0), H))
    half = d // 2
    scale = (np.arange(0, d, 2, dtype=np.float64) + 0.4 * d) / (1.4 * d)
    pos = np.arange(S, dtype=np.float64)
    sc = scale[None, :] ** (pos[:, None] / 512.0)
    inv_freq = 1.0 / 10000.0 ** (np.arange(half, dtype=np.float64) / half)
    sinus = pos[:, None] * inv_freq[None, :]
    sn, cs = np.sin(sinus), np.cos(sinus)
    rep2 = lambda a: np.repeat(a, 2, axis=1)
    cosQ = rep2(cs * sc).T
    sinQ = rep2(sn * sc).T
    cosK = rep2(cs / sc).T
    sinK = rep2(sn / sc).T
    tabs = np.zeros((NCH, 128, 4, C), dtype=np.float64)
    for cc in range(NCH):
        sl = slice(cc * C, (cc + 1) * C)
        tabs[cc, :, 0, :] = cosQ[:, sl]
        tabs[cc, :, 1, :] = sinQ[:, sl]
        tabs[cc, :, 2, :] = cosK[:, sl]
        tabs[cc, :, 3, :] = sinK[:, sl]
    ar = np.arange(C)
    diff = ar[None, :] - ar[:, None]
    dloc = np.zeros((C, H, C), dtype=np.float64)
    for hh in range(H):
        dloc[:, hh, :] = np.where(diff >= 0, g[hh] ** np.maximum(diff, 0), 0.0)
    gqa = np.broadcast_to(g[None, :, None] ** (ar[None, None, :] - C),
                          (128, H, C)).copy()
    gk = (g[None, :] ** (C - ar)[:, None]).astype(np.float32)
    PT = np.zeros((128, 128), dtype=np.float64)
    for j in range(64):
        PT[2 * j + 1, 2 * j] = -1.0
        PT[2 * j, 2 * j + 1] = 1.0
    gct = np.broadcast_to((g ** C).astype(np.float32)[None, :], (128, H)).copy()
    return {
        "tabs": tabs.astype(BF), "dloc": dloc.astype(BF), "gqa": gqa.astype(BF),
        "gk": gk, "gct": gct, "ptm": PT.astype(BF),
        "idn": np.eye(128, dtype=BF), "idn32": np.eye(128, dtype=np.float32),
        "ones1": np.ones((128, 128), dtype=BF),
        "kmagic": np.full((128, 1), 0x5f3759df, dtype=np.int32),
    }


DBG = False
DBGW = False
PARANOID = False


def build_program():
    from contextlib import ExitStack
    nc = bass.Bass("TRN2", target_bir_lowering=False)
    dt = lambda n, s, ty: nc.dram_tensor(n, s, ty, kind="ExternalInput").ap()
    x_d = dt("x", [BPC, S, D], F32)
    wq_d = dt("wq", [H, d, d], F32)
    wk_d = dt("wk", [H, d, d], F32)
    wv_d = dt("wv", [H, d, d], F32)
    wg_d = dt("wgm", [D, D], F32)
    wo_d = dt("wom", [D, D], F32)
    gnw_d = dt("gnw", [D], F32)
    gnb_d = dt("gnb", [D], F32)
    w1_d = dt("w1m", [FH, D], F32)
    b1_d = dt("b1v", [FH], F32)
    w2_d = dt("w2m", [D, FH], F32)
    b2_d = dt("b2v", [D], F32)
    tabs_d = dt("tabs", [NCH, 128, 4, C], BF16)
    dloc_d = dt("dloc", [C, H, C], BF16)
    gqa_d = dt("gqa", [128, H, C], BF16)
    gk_d = dt("gk", [C, H], F32)
    gct_d = dt("gct", [128, H], F32)
    ptm_d = dt("ptm", [128, 128], BF16)
    idn_d = dt("idn", [128, 128], BF16)
    idn32_d = dt("idn32", [128, 128], F32)
    ones_d = dt("ones1", [128, 128], BF16)
    kc_d = dt("kmagic", [128, 1], mybir.dt.int32)
    out_d = nc.dram_tensor("out", [BPC, S, D], F32, kind="ExternalOutput").ap()
    if DBG:
        dbg_d = nc.dram_tensor("dbg", [16, 128, 1024], BF16, kind="ExternalOutput").ap()
        dbgf_d = nc.dram_tensor("dbgf", [4, 128, 1024], BF16, kind="ExternalOutput").ap()
    if DBGW:
        dw1_d = nc.dram_tensor("dw1", [128, H, FH], BF16, kind="ExternalOutput").ap()
        dw2_d = nc.dram_tensor("dw2", [128, FH // 128, D], BF16, kind="ExternalOutput").ap()
        dwg_d = nc.dram_tensor("dwg", [128, H, D], BF16, kind="ExternalOutput").ap()
        dwo_d = nc.dram_tensor("dwo", [128, H, D], BF16, kind="ExternalOutput").ap()

    sem, cnt, waited = {}, {}, {}

    def dinc(inst, k):
        inst.then_inc(sem[k], 16)
        cnt[k] += 16
        return cnt[k]

    def cinc(inst, k):
        inst.then_inc(sem[k], 1)
        cnt[k] += 1
        return cnt[k]

    def wt(eng, ename, k, v=None):
        if v is None:
            v = cnt[k]
        if PARANOID and ename in ("TE", "DV", "AC"):
            for kk in ("TE", "DV", "AC"):
                if kk != ename and cnt[kk] > 0:
                    key2 = (ename, kk)
                    if waited.get(key2, -1) < cnt[kk]:
                        eng.wait_ge(sem[kk], cnt[kk])
                        waited[key2] = cnt[kk]
        if v <= 0:
            return
        key = (ename, k)
        if waited.get(key, -1) < v:
            eng.wait_ge(sem[k], v)
            waited[key] = v

    with ExitStack() as ctx:
        sb = lambda n, s, ty: ctx.enter_context(nc.sbuf_tensor(n, s, ty))
        w1t = sb("w1t", [128, H, FH], BF16)
        w2s = sb("w2s", [128, FH // 128, D], BF16)
        wg = sb("wg", [128, H, D], BF16)
        wo = sb("wo", [128, H, D], BF16)
        wq = sb("wqs", [128, H, d], BF16)
        wk = sb("wks", [128, H, d], BF16)
        wv = sb("wvs", [128, H, d], BF16)
        dloc = sb("dlocs", [C, H, C], BF16)
        gqa = sb("gqas", [128, H, C], BF16)
        gk = sb("gks", [C, H], F32)
        gct = sb("gcts", [128, H], F32)
        ptm = sb("ptms", [128, 128], BF16)
        idn = sb("idns", [128, 128], BF16)
        idn32 = sb("idn32s", [128, 128], F32)
        ones = sb("oness", [128, 128], BF16)
        kcs = sb("kcss", [128, 1], mybir.dt.int32)
        gnw = sb("gnws", [128, H], F32)
        gnb = sb("gnbs", [128, H], F32)
        bia = sb("biab", [128, 2048], BF16)
        bslot = lambda s_: ((s_ % 3) * 32, (s_ // 3) * 512)
        rbf = sb("rbfs", [128, H, 128], BF16)
        tab = sb("tabb", [128, 4, C], BF16)
        xv = sb("xvb", [128, D], F32)
        st1 = sb("st1b", [128, 4], F32)
        gs = sb("gsb", [128, 32], F32)

        vpl = sb("vpl", [128, 1], F32)
        vhl = sb("vhl", [128, 1], F32)
        rl = sb("rl", [128, 1], F32)
        al = sb("al", [128, 1], F32)
        sil = sb("sil", [128, 1], mybir.dt.int32)
        vpg = sb("vpg", [128, 8], F32)
        vhg = sb("vhg", [128, 8], F32)
        rg = sb("rg", [128, 8], F32)
        ag = sb("ag", [128, 8], F32)
        sig = sb("sig", [128, 8], mybir.dt.int32)
        P0 = ctx.enter_context(nc.psum_tensor("P0", [128, 1024], F32))
        P1 = ctx.enter_context(nc.psum_tensor("P1", [128, 1024], F32))
        P2 = ctx.enter_context(nc.psum_tensor("P2", [128, 1024], F32))
        PTp = ctx.enter_context(nc.psum_tensor("PTp", [128, 1024], BF16))

        for s_name in ["LX", "LT", "LW", "LA", "LB", "LS", "S0", "S1", "SO", "SP", "DB", "TE", "DV", "AC"]:
            sem[s_name] = ctx.enter_context(nc.semaphore(s_name))
            cnt[s_name] = 0

        block = ctx.enter_context(nc.Block())

        P0v = P0[:].rearrange("p (h e) -> p h e", h=H)
        P1v = P1[:].rearrange("p (h e) -> p h e", h=H)
        P2v = P2[:].rearrange("p (h e) -> p h e", h=H)
        PTv = PTp[:].rearrange("p (h e) -> p h e", h=H)
        flat = lambda t: t[:].rearrange("p h e -> p (h e)")
        gvals = 1.0 - np.exp(np.linspace(math.log(1 / 32.0),
                                         math.log(1 / 512.0), H))
        gC = [float(gv_ ** C) for gv_ in gvals]

        I32 = mybir.dt.int32
        lnsum, lnsq = st1[:, 0:1], st1[:, 1:2]
        negmu, e2 = st1[:, 2:3], st1[:, 3:4]
        varp, vh, r_, a_ = vpl[:], vhl[:], rl[:], al[:]
        mu2 = vh
        bp = e2
        shi = sil[:]
        g_negsum, g_sq = gs[:, 0:8], gs[:, 8:16]
        g_negmu, g_e2 = gs[:, 16:24], gs[:, 24:32]
        g_varp, g_vh, g_r, g_a = vpg[:], vhg[:], rg[:], ag[:]
        g_mu2 = g_vh
        g_bp = g_e2
        g_shi = sig[:]

        def gv(v, inst):
            wt(v, "DV", "DV", cinc(inst, "DV"))

        def rsqrt_dve(v, varp_ap, vh_ap, r_ap, a_ap, shi_ap, nsl):
            iv = v.tensor_scalar_mul(vh_ap, varp_ap, -0.5)
            gv(v, iv)
            kb = kcs[:] if nsl == 1 else kcs[:].broadcast_to([128, nsl])
            iv = v.tensor_single_scalar(shi_ap, varp_ap.bitcast(I32),
                                        1, ALU.arith_shift_right)
            gv(v, iv)
            iv = v.tensor_sub(r_ap.bitcast(I32), kb, shi_ap)
            gv(v, iv)
            for _ in range(2):
                iv = v.tensor_mul(a_ap, r_ap, r_ap)
                gv(v, iv)
                iv = v.tensor_tensor(a_ap, a_ap, vh_ap, ALU.mult)
                gv(v, iv)
                iv = v.tensor_scalar_add(a_ap, a_ap, 1.5)
                gv(v, iv)
                iv = v.tensor_mul(r_ap, r_ap, a_ap)
                gv(v, iv)

        @block.sync
        def _(sy):
            dinc(sy.dma_start(out=xv[:], in_=x_d[0, 0:C, :]), "LX")
            dinc(sy.dma_start(out=tab[:], in_=tabs_d[0]), "LT")
            dinc(sy.dma_start(out=gnw[:], in_=gnw_d.rearrange("(h e) -> e h", h=H), allow_slow_non_contiguous=True), "LS")
            dinc(sy.dma_start(out=gnb[:], in_=gnb_d.rearrange("(h e) -> e h", h=H), allow_slow_non_contiguous=True), "LS")
            dinc(sy.dma_start(out=gk[:], in_=gk_d), "LS")
            dinc(sy.dma_start(out=gct[:], in_=gct_d), "LS")
            dinc(sy.dma_start(out=idn32[:], in_=idn32_d), "LS")
            dinc(sy.dma_start(out=kcs[:], in_=kc_d), "LS")
        LS_CONSTS = cnt["LS"]

        @block.gpsimd
        def _(g):
            dinc(g.dma_start(out=idn[:], in_=idn_d), "LW")
        LW_IDN = cnt["LW"]

        @block.gpsimd
        def _(g):
            for dst, src in [
                (dloc[:], dloc_d), (gqa[:], gqa_d),
                (ptm[:], ptm_d), (ones[:], ones_d),
                (wq[:], wq_d.rearrange("h a e -> a h e")),
                (wk[:], wk_d.rearrange("h a e -> a h e")),
                (wv[:], wv_d.rearrange("h a e -> a h e")),
                (wg[:], wg_d.rearrange("(k p) n -> p k n", p=128)),
                (wo[:], wo_d.rearrange("(k p) n -> p k n", p=128)),
            ] + [
                (bia[bslot(s_)[0]:bslot(s_)[0] + 1,
                     bslot(s_)[1]:bslot(s_)[1] + 512],
                 (b1_d if s_ < 8 else b2_d)[(s_ % 8) * 512:(s_ % 8) * 512 + 512]
                 .rearrange("(a n) -> a n", a=1))
                for s_ in range(10)

            ]:
                dinc(g.dma_start(out=dst, in_=src), "LW")
        LW_ALLC = cnt["LW"]

        with ExitStack() as pctx:
            psb = lambda n, s, ty: pctx.enter_context(nc.sbuf_tensor(n, s, ty))
            wstrip = psb("wstrip", [128, 2, 1024], F32)
            wstrip2 = psb("wstrip2", [128, 2, 1024], BF16)
            sgn = psb("sgnb", [128, 1024], BF16)
            s1b = psb("s1bb", [128, 1024], BF16)
            abuf = psb("abufb", [128, 2, 512], F32)
            a1t = psb("a1tb", [128, 1], F32)
            a2s = psb("a2sb", [128, 8], F32)
            a2p = psb("a2pb", [128, 8], F32)

            dv_alpha_base = cnt["DV"]

            def _a2_incs_before(j):
                return j + j // 8

            alpha_cnt = []

            @block.scalar
            def _(sc):
                for i in range(64):
                    if i >= 2:
                        wt(sc, "AC", "DV", dv_alpha_base + _a2_incs_before(i - 2) + 1)
                    sd, pp = divmod(i, 8)
                    dd = sc.dma_start(
                        out=abuf[:, i % 2, :],
                        in_=w2_d[sd * 128:(sd + 1) * 128, pp * 512:(pp + 1) * 512])
                    key = "LA" if i % 2 == 0 else "LB"
                    alpha_cnt.append((key, dinc(dd, key)))

            for i in range(64):
                sd, pp = divmod(i, 8)

                @block.vector
                def _(v, i=i, sd=sd, pp=pp):
                    wt(v, "DV", *alpha_cnt[i])
                    iv = v.reduce_sum(a2p[:, pp:pp + 1], abuf[:, i % 2, :], AX.X,
                                      apply_absolute_value=True)
                    if pp == 7:
                        wt(v, "DV", "DV", cinc(iv, "DV"))
                        iv = v.tensor_reduce(a2s[:, sd:sd + 1], a2p[:], AX.X, ALU.add)
                    cinc(iv, "DV")

            @block.vector
            def _(v):
                wt(v, "DV", "DV")
                iv = v.tensor_scalar_mul(a2s[:], a2s[:], 1.0 / FH)
                cinc(iv, "DV")
            DV_ALPHA2 = cnt["DV"]
            if DBG:
                @block.gpsimd
                def _(g):
                    wt(g, "GP", "DV", DV_ALPHA2)
                    dinc(g.dma_start(out=dbgf_d[3, :, 8:16], in_=a2s[:],
                                     allow_slow_non_contiguous=True), "DB")

            ac_w1_base = cnt["AC"]
            w1_cnt = []

            @block.sync
            def _(sy):
                for s_ in range(32):
                    if s_ >= 2:
                        wt(sy, "SY", "AC", ac_w1_base + (s_ - 1))
                        wt(sy, "SY", "DV", DV_ALPHA2 + 3 * (s_ - 2) + 1)
                    dd = sy.dma_start(out=wstrip[:, s_ % 2, :],
                                      in_=w1_d[s_ * 128:(s_ + 1) * 128, :])
                    key = "S0" if s_ % 2 == 0 else "S1"
                    w1_cnt.append((key, dinc(dd, key)))

            for s_ in range(32):
                @block.vector
                def _(v, s_=s_):
                    wt(v, "DV", *w1_cnt[s_])
                    iv = v.reduce_sum(a1t[:], wstrip[:, s_ % 2, :], AX.X,
                                      apply_absolute_value=True)
                    wt(v, "DV", "DV", cinc(iv, "DV"))
                    iv = v.tensor_scalar_mul(a1t[:], a1t[:], 1.0 / D)
                    cinc(iv, "DV")

                @block.scalar
                def _(sc, s_=s_):
                    wt(sc, "AC", "DV")
                    wt(sc, "AC", "TE")
                    wt(sc, "AC", *w1_cnt[s_])
                    sc.sign(sgn[:], wstrip[:, s_ % 2, :])
                    ia = sc.mul(s1b[:], sgn[:], a1t[:])
                    cinc(ia, "AC")

                @block.tensor
                def _(te, s_=s_):
                    wt(te, "TE", "AC")
                    wt(te, "TE", "LW", LW_ALLC)
                    if s_ > 0:
                        wt(te, "TE", "DV")
                    for kf in range(H):
                        it = te.transpose(PTv[:, kf, :],
                                          s1b[:, kf * 128:(kf + 1) * 128], idn[:])
                    cinc(it, "TE")

                @block.vector
                def _(v, s_=s_):
                    wt(v, "DV", "TE")
                    iv = v.tensor_copy(w1t[:, :, s_ * 128:(s_ + 1) * 128], PTv)
                    cinc(iv, "DV")
            DV_W1 = cnt["DV"]

            ac_w2_base = cnt["AC"]
            w2_cnt = []

            @block.gpsimd
            def _(g):
                for i in range(32):
                    if i >= 2:
                        wt(g, "GP", "AC", ac_w2_base + (i - 1))
                    sd, pc = divmod(i, 4)
                    dd = g.dma_start(
                        out=wstrip2[:, i % 2, :],
                        in_=w2_d[sd * 128:(sd + 1) * 128, pc * 1024:(pc + 1) * 1024])
                    key = "S0" if i % 2 == 0 else "S1"
                    w2_cnt.append((key, dinc(dd, key)))

            for i in range(32):
                sd, pc = divmod(i, 4)

                @block.scalar
                def _(sc, i=i, sd=sd):
                    wt(sc, "AC", *w2_cnt[i])
                    wt(sc, "AC", "DV", DV_ALPHA2)
                    wt(sc, "AC", "TE")
                    sc.sign(sgn[:], wstrip2[:, i % 2, :])
                    ia = sc.mul(s1b[:], sgn[:], a2s[:, sd:sd + 1])
                    cinc(ia, "AC")

                @block.tensor
                def _(te, i=i):
                    wt(te, "TE", "AC")
                    wt(te, "TE", "DV")
                    for j in range(H):
                        it = te.transpose(PTv[:, j, :],
                                          s1b[:, j * 128:(j + 1) * 128], idn[:])
                    cinc(it, "TE")

                @block.vector
                def _(v, i=i, sd=sd, pc=pc):
                    wt(v, "DV", "TE")
                    iv = v.tensor_copy(
                        w2s[:, pc * 8:(pc + 1) * 8, sd * 128:(sd + 1) * 128], PTv)
                    cinc(iv, "DV")
        DV_PRO = cnt["DV"]
        AC_PRO = cnt["AC"]
        TE_PRO = cnt["TE"]
        LS_PRO = cnt["LS"]
        LA_PRO = cnt["LA"]

        oub = sb("oub", [128, 512], F32)
        oub2 = sb("oub2", [128, 512], F32)
        hib = sb("hib", [128, D], BF16)
        lob = sb("lob", [128, D], BF16)
        xnf = sb("xnfb", [128, H, C], BF16)
        qbf = sb("qbfb", [128, H, C], BF16)
        kbf = sb("kbfb", [128, H, C], BF16)
        vtm = sb("vtmb", [128, H, d], BF16)
        sA = sb("sAb", [128, H, C], BF16)
        sB = sb("sBb", [128, H, C], BF16)
        sC = sb("sCb", [128, H, C], BF16)
        sD = sb("sDb", [128, H, C], BF16)

        prev = {"imm": 0, "silu": 0, "gelu3": 0, "oevac": 0, "hfm2": DV_PRO,
                "xpk": 0}
        pend = {"dv_xn": 0, "dv_split": 0}

        def ln_section(cj, gate):
            @block.vector
            def _(v, cj=cj, gate=gate):
                if gate:
                    wt(v, "DV", "LW", LW_ALLC)
                    wt(v, "DV", "LS", LS_PRO)
                    wt(v, "DV", "LA", LA_PRO)
                    wt(v, "DV", "TE", TE_PRO)
                    wt(v, "DV", "AC", AC_PRO)
                wt(v, "DV", "LX", 16 * (cj + 1))
                iv = v.reduce_sum(lnsum, xv[:], AX.X, negate=True)
                gv(v, iv)
                iv = v.scalar_tensor_tensor(flat(qbf), xv[:], 1.0, xv[:],
                                            ALU.bypass, ALU.mult, accum_out=lnsq)
                gv(v, iv)
                iv = v.tensor_scalar_mul(negmu, lnsum, 1.0 / D)
                gv(v, iv)
                iv = v.tensor_scalar(e2, lnsq, 1.0 / D, 1e-5, ALU.mult, ALU.add)
                gv(v, iv)
                iv = v.tensor_mul(mu2, negmu, negmu)
                gv(v, iv)
                iv = v.tensor_sub(varp, e2, mu2)
                gv(v, iv)
                rsqrt_dve(v, varp, vh, r_, a_, shi, 1)
                iv = v.tensor_scalar(flat(sD), xv[:], negmu, None, ALU.add)
                gv(v, iv)
                iv = v.tensor_scalar(flat(sD), flat(sD), r_, None, ALU.mult)
                cinc(iv, "DV")
            pend["dv_xn"] = cnt["DV"]

        ln_section(0, True)

        @block.vector
        def _(v):
            iv = v.tensor_copy(hib[:], xv[:])
            cinc(iv, "DV")
            iv = v.tensor_tensor(lob[:], xv[:], hib[:], ALU.subtract)
            pend["dv_split"] = cinc(iv, "DV")

        for ci in range(NCHUNKS):
            seq, cc = divmod(ci, NCH)
            first = (cc == 0)

            @block.tensor
            def _(te, ci=ci):
                if ci == 0:
                    wt(te, "TE", "LS", LS_PRO)
                    wt(te, "TE", "AC", AC_PRO)
                wt(te, "TE", "DV", pend["dv_xn"])
                wt(te, "TE", "DV", prev["hfm2"])
                for kf in range(H):
                    it = te.transpose(PTv[:, kf, :], sD[:, kf, :], idn[:])
                cinc(it, "TE")
            te_xnT = cnt["TE"]

            @block.vector
            def _(v):
                wt(v, "DV", "TE", te_xnT)
                iv = v.tensor_copy(flat(xnf), PTp[:])
                cinc(iv, "DV")
            dv_xnf = cnt["DV"]

            @block.tensor
            def _(te):
                wt(te, "TE", "DV", dv_xnf)
                wt(te, "TE", "AC", prev["silu"])
                for h in range(H):
                    iq = te.matmul(P0v[:, h, :], wq[:, h, :], xnf[:, h, :],
                                   start=True, stop=True)
                cinc(iq, "TE")
            te_q = cnt["TE"]

            if ci > 0:
                @block.tensor
                def _(te):
                    wt(te, "TE", "AC", prev["gelu3"])
                    for j in range(H):
                        it = te.transpose(PTv[:, j, :], sA[:, j, :], idn[:])
                    cinc(it, "TE")
                te_hT3p = cnt["TE"]

            @block.tensor
            def _(te):
                wt(te, "TE", "AC", prev["gelu3"])
                for h in range(H):
                    ik = te.matmul(P2v[:, h, :], wk[:, h, :], xnf[:, h, :],
                                   start=True, stop=True)
                cinc(ik, "TE")
            te_k = cnt["TE"]

            @block.vector
            def _(v):
                wt(v, "DV", "TE", te_q)
                iv = v.tensor_copy(flat(qbf), P0[:])
                cinc(iv, "DV")
            dv_qe = cnt["DV"]

            if ci > 0:
                @block.vector
                def _(v):
                    wt(v, "DV", "TE", te_hT3p)
                    iv = v.tensor_copy(flat(sC), PTp[:])
                    cinc(iv, "DV")
                dv_hfm3p = cnt["DV"]

            @block.vector
            def _(v):
                wt(v, "DV", "TE", te_k)
                iv = v.tensor_copy(flat(kbf), P2[:])
                cinc(iv, "DV")
            dv_ke = cnt["DV"]

            @block.tensor
            def _(te):
                wt(te, "TE", "DV", dv_qe)
                for h in range(H):
                    ir = te.matmul(P0v[:, h, :], ptm[:], qbf[:, h, :],
                                   start=True, stop=True)
                cinc(ir, "TE")
                wt(te, "TE", "DV", dv_ke)
                for h in range(H):
                    ir = te.matmul(P2v[:, h, :], ptm[:], kbf[:, h, :],
                                   start=True, stop=True)
                cinc(ir, "TE")
            te_rk = cnt["TE"]
            te_rq = te_rk - 1

            if ci > 0:
                @block.tensor
                def _(te):
                    wt(te, "TE", "DV", dv_hfm3p)
                    for j in range(H):
                        for nh in range(2):
                            iw = te.matmul(P1[:, nh * 512:(nh + 1) * 512],
                                           sC[:, j, :],
                                           w2s[:, 24 + j, nh * 512:(nh + 1) * 512],
                                           start=False, stop=False,
                                           skip_group_check=True)
                    for nh in range(2):
                        ib = te.matmul(
                            P1[:, nh * 512:(nh + 1) * 512],
                            ones[bslot(8 + nh)[0]:bslot(8 + nh)[0] + 1, :],
                            bia[bslot(8 + nh)[0]:bslot(8 + nh)[0] + 1,
                                bslot(8 + nh)[1]:bslot(8 + nh)[1] + 512],
                            start=False, stop=True, skip_group_check=True)
                    cinc(ib, "TE")
                te_b2p = cnt["TE"]

            cosQ = tab[:, 0:1, :].broadcast_to([128, H, C])
            sinQ = tab[:, 1:2, :].broadcast_to([128, H, C])
            cosK = tab[:, 2:3, :].broadcast_to([128, H, C])
            sinK = tab[:, 3:4, :].broadcast_to([128, H, C])

            @block.vector
            def _(v, ci=ci):
                wt(v, "DV", "LT", 16 * (ci + 1))
                wt(v, "DV", "TE", te_rq)
                iv = v.tensor_tensor(sA[:], P0v, sinQ, ALU.mult)
                cinc(iv, "DV")
            dv_rqe = cnt["DV"]

            @block.tensor
            def _(te):
                wt(te, "TE", "DV", dv_rqe)
                for h in range(H):
                    iv_ = te.matmul(P0v[:, h, :], xnf[:, h, :], wv[:, h, :],
                                    start=True, stop=True)
                cinc(iv_, "TE")
            te_v = cnt["TE"]

            @block.vector
            def _(v, ci=ci, first=first):
                v.tensor_mul(sB[:], qbf[:], cosQ)
                iv = v.tensor_add(qbf[:], sA[:], sB[:])
                cinc(iv, "DV")
                wt(v, "DV", "TE", te_rk)
                iv = v.tensor_tensor(sB[:], P2v, sinK, ALU.mult)
                cinc(iv, "DV")
                v.tensor_mul(sA[:], kbf[:], cosK)
                iv = v.tensor_add(kbf[:], sA[:], sB[:])
                prev["xpk"] = cinc(iv, "DV")
                if not first:
                    iv = v.tensor_mul(sD[:], qbf[:], gqa[:])
                    cinc(iv, "DV")
                wt(v, "DV", "TE", te_v)
                iv = v.tensor_copy(flat(vtm), P0[:])
                cinc(iv, "DV")
            dv_kx = prev["xpk"]
            dv_ve = cnt["DV"]

            if ci + 1 < NCHUNKS:
                ncc2 = (ci + 1) % NCH

                @block.sync
                def _(sy, ncc2=ncc2):
                    wt(sy, "SY", "DV", prev["xpk"])
                    dd = sy.dma_start(out=tab[:], in_=tabs_d[ncc2])
                    dinc(dd, "LT")

            @block.tensor
            def _(te):
                wt(te, "TE", "DV", dv_kx)
                for h in range(H):
                    it = te.transpose(PTv[:, h, :], kbf[:, h, :], idn[:])
                cinc(it, "TE")
                for h in range(H):
                    ia_ = te.matmul(P2v[:, h, :], kbf[:, h, :], qbf[:, h, :],
                                    start=True, stop=True)
                cinc(ia_, "TE")
            te_a = cnt["TE"]
            te_ksT = te_a - 1

            @block.vector
            def _(v):
                wt(v, "DV", "TE", te_a)
                iv = v.tensor_tensor(sA[:], P2v, dloc[:], ALU.mult)
                cinc(iv, "DV")
                wt(v, "DV", "TE", te_ksT)
                gkb = gk[:].unsqueeze(2).broadcast_to([128, H, 128])
                iv = v.tensor_tensor(sC[:], PTv, gkb, ALU.mult)
                cinc(iv, "DV")
            dv_ks = cnt["DV"]
            dv_ad = dv_ks - 1

            if ci > 0:
                @block.vector
                def _(v, ci=ci):
                    wt(v, "DV", "TE", te_b2p)
                    wt(v, "DV", "SO", 16 * (ci - 1))
                    iv = v.tensor_copy(oub[:], P1[:, 0:512])
                    cinc(iv, "DV")
                    wt(v, "DV", "SP", 16 * (ci - 1))
                    iv = v.tensor_copy(oub2[:], P1[:, 512:1024])
                    prev["oevac"] = cinc(iv, "DV")
                dv_oe = prev["oevac"]
                pseq, pcc = divmod(ci - 1, NCH)

                @block.scalar
                def _(sc, pseq=pseq, pcc=pcc, dv_oe=dv_oe):
                    wt(sc, "AC", "DV", dv_oe - 1)
                    dd = sc.dma_start(
                        out=out_d[pseq, pcc * 128:(pcc + 1) * 128, 0:512],
                        in_=oub[:])
                    dinc(dd, "SO")

                @block.sync
                def _(sy, pseq=pseq, pcc=pcc, dv_oe=dv_oe):
                    wt(sy, "SY", "DV", dv_oe)
                    dd = sy.dma_start(
                        out=out_d[pseq, pcc * 128:(pcc + 1) * 128, 512:1024],
                        in_=oub2[:])
                    dinc(dd, "SP")

            @block.tensor
            def _(te, first=first):
                wt(te, "TE", "DV", dv_ad)
                wt(te, "TE", "DV", dv_ve)
                for h in range(H):
                    iy = te.matmul(P0v[:, h, :], sA[:, h, :], vtm[:, h, :],
                                   start=True, stop=first)
                    if not first:
                        iy = te.matmul(P0v[:, h, :], sD[:, h, :], rbf[:, h, :],
                                       start=False, stop=True)
                cinc(iy, "TE")
                wt(te, "TE", "DV", dv_ks)
                for h in range(H):
                    is_ = te.matmul(P2v[:, h, :], sC[:, h, :], vtm[:, h, :],
                                    start=True, stop=first)
                    if not first:
                        is_ = te.matmul(P2v[:, h, :], idn[:], rbf[:, h, :],
                                        start=False, stop=True)
                cinc(is_, "TE")
            te_s = cnt["TE"]
            te_y = te_s - 1

            @block.tensor
            def _(te):
                wt(te, "TE", "DV", prev["oevac"])
                wt(te, "TE", "DV", pend["dv_split"])
                for nh in range(2):
                    im = te.matmul(P1[:, nh * 512:(nh + 1) * 512], idn[:],
                                   hib[:, nh * 512:(nh + 1) * 512],
                                   start=True, stop=False, skip_group_check=True)
                    im = te.matmul(P1[:, nh * 512:(nh + 1) * 512], idn[:],
                                   lob[:, nh * 512:(nh + 1) * 512],
                                   start=False, stop=False, skip_group_check=True)
                prev["imm"] = cinc(im, "TE")

            if ci + 1 < NCHUNKS:
                nseq, ncc = divmod(ci + 1, NCH)

                @block.sync
                def _(sy, nseq=nseq, ncc=ncc):
                    wt(sy, "SY", "TE", prev["imm"])
                    dd = sy.dma_start(out=xv[:],
                                      in_=x_d[nseq, ncc * 128:(ncc + 1) * 128, :])
                    dinc(dd, "LX")

            @block.vector
            def _(v):
                wt(v, "DV", "TE", te_s)
                gcb = gct[:].unsqueeze(2).broadcast_to([128, H, 128])
                iv = v.tensor_tensor(rbf[:], P2v, gcb, ALU.mult)
                cinc(iv, "DV")
            dv_rbf = cnt["DV"]

            @block.vector
            def _(v):
                iv = v.tensor_copy(flat(qbf), P0[:])
                gv(v, iv)
                prev["p0f"] = cnt["DV"]
                qbv = qbf[:]
                iv = v.reduce_sum(g_negsum, qbv, AX.X, negate=True)
                gv(v, iv)
                iv = v.tensor_tensor(flat(kbf), qbf[:].rearrange("p h e -> p (h e)"),
                                     flat(qbf), ALU.mult)
                gv(v, iv)
                iv = v.tensor_reduce(g_sq, kbf[:], AX.X, ALU.add)
                gv(v, iv)
                iv = v.tensor_scalar_mul(g_negmu, g_negsum, 1.0 / 128.0)
                gv(v, iv)
                iv = v.tensor_scalar(g_e2, g_sq, 1.0 / 128.0, 1e-5, ALU.mult, ALU.add)
                gv(v, iv)
                iv = v.tensor_mul(g_mu2, g_negmu, g_negmu)
                gv(v, iv)
                iv = v.tensor_sub(g_varp, g_e2, g_mu2)
                gv(v, iv)
                rsqrt_dve(v, g_varp, g_vh, g_r, g_a, g_shi, 8)
                nmb = gs[:, 16:24].unsqueeze(2).broadcast_to([128, H, 128])
                grb = rg[:].unsqueeze(2).broadcast_to([128, H, 128])
                iv = v.scalar_tensor_tensor(qbf[:], qbf[:], 1.0, nmb,
                                            ALU.bypass, ALU.add)
                gv(v, iv)
                iv = v.tensor_tensor(sB[:], qbf[:], grb, ALU.mult)
                cinc(iv, "DV")
            dv_ynp = cnt["DV"]

            @block.tensor
            def _(te, ci=ci):
                wt(te, "TE", "DV", dv_rbf)
                if ci == 0:
                    wt(te, "TE", "DV", DV_PRO)
                for kf in range(H):
                    for nh in range(2):
                        ih = te.matmul(P2[:, nh * 512:(nh + 1) * 512],
                                       xnf[:, kf, :],
                                       w1t[:, kf, nh * 512:nh * 512 + 512],
                                       start=(kf == 0), stop=False)
                for nh in range(2):
                    ih = te.matmul(P2[:, nh * 512:(nh + 1) * 512],
                                   ones[bslot(nh)[0]:bslot(nh)[0] + 1, :],
                                   bia[bslot(nh)[0]:bslot(nh)[0] + 1,
                                       bslot(nh)[1]:bslot(nh)[1] + 512],
                                   start=False, stop=True)
                cinc(ih, "TE")
            te_h = [cnt["TE"], 0, 0, 0]

            @block.scalar
            def _(sc, ci=ci):
                if ci == 0:
                    wt(sc, "AC", "LS", LS_PRO)
                    wt(sc, "AC", "LA", LA_PRO)
                    wt(sc, "AC", "DV", DV_PRO)
                wt(sc, "AC", "TE", te_h[0])
                ia = sc.activation(flat(sA), P2[:], AF.Gelu)
                cinc(ia, "AC")
            ac_gel = [cnt["AC"], 0, 0, 0]

            @block.tensor
            def _(te):
                wt(te, "TE", "DV", prev["p0f"])
                for kf in range(H):
                    for nh in range(2):
                        ig = te.matmul(P0[:, nh * 512:(nh + 1) * 512],
                                       xnf[:, kf, :],
                                       wg[:, kf, nh * 512:(nh + 1) * 512],
                                       start=(kf == 0), stop=(kf == H - 1))
                cinc(ig, "TE")
            te_g = cnt["TE"]

            @block.scalar
            def _(sc):
                wt(sc, "AC", "TE", te_g)
                ia = sc.activation(flat(vtm), P0[:], AF.Silu)
                prev["silu"] = cinc(ia, "AC")
            ac_silu = cnt["AC"]

            @block.tensor
            def _(te):
                wt(te, "TE", "AC", ac_gel[0])
                wt(te, "TE", "DV", dv_ks)
                for j in range(H):
                    it = te.transpose(PTv[:, j, :], sA[:, j, :], idn[:])
                cinc(it, "TE")
                for kf in range(H):
                    for nh in range(2):
                        ih = te.matmul(P2[:, nh * 512:(nh + 1) * 512],
                                       xnf[:, kf, :],
                                       w1t[:, kf, 1024 + nh * 512:1024 + nh * 512 + 512],
                                       start=(kf == 0), stop=False)
                for nh in range(2):
                    ih = te.matmul(P2[:, nh * 512:(nh + 1) * 512],
                                   ones[bslot(2 + nh)[0]:bslot(2 + nh)[0] + 1, :],
                                   bia[bslot(2 + nh)[0]:bslot(2 + nh)[0] + 1,
                                       bslot(2 + nh)[1]:bslot(2 + nh)[1] + 512],
                                   start=False, stop=True)
                cinc(ih, "TE")
            te_h[1] = cnt["TE"]
            te_hT0 = te_h[1] - 1

            @block.vector
            def _(v):
                wt(v, "DV", "TE", te_hT0)
                iv = v.tensor_copy(flat(sC), PTp[:])
                cinc(iv, "DV")
            dv_hfm = [cnt["DV"], 0, 0, 0]

            @block.scalar
            def _(sc, ci=ci):
                wt(sc, "AC", "TE", te_h[1])
                ia = sc.activation(flat(sA), P2[:], AF.Gelu)
                cinc(ia, "AC")
            ac_gel[1] = cnt["AC"]

            @block.tensor
            def _(te):
                wt(te, "TE", "DV", dv_ynp)
                wt(te, "TE", "DV", dv_hfm[0])
                for h in range(H):
                    it = te.transpose(PTv[:, h, :], sB[:, h, :], idn[:])
                cinc(it, "TE")
            te_ynT = cnt["TE"]

            @block.vector
            def _(v):
                wt(v, "DV", "TE", te_ynT)
                gwb = gnw[:].unsqueeze(2).broadcast_to([128, H, 128])
                gbb = gnb[:].unsqueeze(2).broadcast_to([128, H, 128])
                iv = v.scalar_tensor_tensor(kbf[:], PTv, 1.0, gwb, ALU.bypass, ALU.mult)
                gv(v, iv)
                iv = v.tensor_tensor(kbf[:], kbf[:], gbb, ALU.add)
                cinc(iv, "DV")
            dv_ynf = cnt["DV"]

            @block.tensor
            def _(te):
                wt(te, "TE", "DV", dv_hfm[0])
                for j in range(H):
                    for nh in range(2):
                        iw = te.matmul(P1[:, nh * 512:(nh + 1) * 512],
                                       sC[:, j, :],
                                       w2s[:, j, nh * 512:(nh + 1) * 512],
                                       start=False, stop=False,
                                       skip_group_check=True)
                cinc(iw, "TE")
                wt(te, "TE", "AC", ac_silu)
                wt(te, "TE", "DV", dv_ynf)
                for h in range(H):
                    it = te.transpose(PTv[:, h, :], vtm[:, h, :], idn[:])
                cinc(it, "TE")
            te_sgT = cnt["TE"]
            te_w2_0 = te_sgT - 1

            @block.vector
            def _(v):
                wt(v, "DV", "TE", te_sgT)
                iv = v.tensor_tensor(sB[:], PTv, kbf[:], ALU.mult)
                cinc(iv, "DV")
            dv_prod = cnt["DV"]

            if ci + 1 < NCHUNKS:
                ln_section(ci + 1, False)

                @block.vector
                def _(v):
                    wt(v, "DV", "TE", prev["imm"])
                    iv = v.tensor_copy(hib[:], xv[:])
                    cinc(iv, "DV")
                    iv = v.tensor_tensor(lob[:], xv[:], hib[:], ALU.subtract)
                    pend["dv_split"] = cinc(iv, "DV")

            @block.tensor
            def _(te):
                wt(te, "TE", "DV", dv_prod)
                for kf in range(H):
                    for nh in range(2):
                        io = te.matmul(P1[:, nh * 512:(nh + 1) * 512],
                                       sB[:, kf, :],
                                       wo[:, kf, nh * 512:(nh + 1) * 512],
                                       start=False, stop=False,
                                       skip_group_check=True)
                cinc(io, "TE")
            te_wo = cnt["TE"]

            for hg in (2, 3):
                @block.tensor
                def _(te, hg=hg):
                    wt(te, "TE", "AC", ac_gel[hg - 1])
                    wt(te, "TE", "DV", dv_prod if hg == 2 else dv_hfm[1])
                    for j in range(H):
                        it = te.transpose(PTv[:, j, :], sA[:, j, :], idn[:])
                    cinc(it, "TE")
                    for kf in range(H):
                        for nh in range(2):
                            ih = te.matmul(P2[:, nh * 512:(nh + 1) * 512],
                                           xnf[:, kf, :],
                                           w1t[:, kf, hg * 1024 + nh * 512:
                                               hg * 1024 + nh * 512 + 512],
                                           start=(kf == 0), stop=False)
                    for nh in range(2):
                        ih = te.matmul(P2[:, nh * 512:(nh + 1) * 512],
                                       ones[bslot(2 * hg + nh)[0]:
                                            bslot(2 * hg + nh)[0] + 1, :],
                                       bia[bslot(2 * hg + nh)[0]:
                                           bslot(2 * hg + nh)[0] + 1,
                                           bslot(2 * hg + nh)[1]:
                                           bslot(2 * hg + nh)[1] + 512],
                                       start=False, stop=True)
                    cinc(ih, "TE")
                te_h[hg] = cnt["TE"]
                te_hTprev = te_h[hg] - 1

                @block.vector
                def _(v, hg=hg, te_hTprev=te_hTprev):
                    wt(v, "DV", "TE", te_hTprev)
                    iv = v.tensor_copy(flat(sC), PTp[:])
                    cinc(iv, "DV")
                dv_hfm[hg - 1] = cnt["DV"]

                @block.scalar
                def _(sc, hg=hg):
                    wt(sc, "AC", "TE", te_h[hg])
                    ia = sc.activation(flat(sA), P2[:], AF.Gelu)
                    cinc(ia, "AC")
                ac_gel[hg] = cnt["AC"]

                @block.tensor
                def _(te, hg=hg):
                    wt(te, "TE", "DV", dv_hfm[hg - 1])
                    for j in range(H):
                        for nh in range(2):
                            iw = te.matmul(P1[:, nh * 512:(nh + 1) * 512],
                                           sC[:, j, :],
                                           w2s[:, (hg - 1) * 8 + j,
                                               nh * 512:(nh + 1) * 512],
                                           start=False, stop=False,
                                           skip_group_check=True)
                    cinc(iw, "TE")
            prev["gelu3"] = ac_gel[3]
            prev["hfm2"] = dv_hfm[2]

        lci = NCHUNKS - 1
        lseq, lcc = divmod(lci, NCH)

        @block.tensor
        def _(te):
            wt(te, "TE", "AC", prev["gelu3"])
            wt(te, "TE", "DV", prev["hfm2"])
            for j in range(H):
                it = te.transpose(PTv[:, j, :], sA[:, j, :], idn[:])
            cinc(it, "TE")
        te_hT3p = cnt["TE"]

        @block.vector
        def _(v):
            wt(v, "DV", "TE", te_hT3p)
            iv = v.tensor_copy(flat(sC), PTp[:])
            cinc(iv, "DV")
        dv_hfm3p = cnt["DV"]

        @block.tensor
        def _(te):
            wt(te, "TE", "DV", dv_hfm3p)
            for j in range(H):
                for nh in range(2):
                    iw = te.matmul(P1[:, nh * 512:(nh + 1) * 512],
                                   sC[:, j, :],
                                   w2s[:, 24 + j, nh * 512:(nh + 1) * 512],
                                   start=False, stop=False,
                                   skip_group_check=True)
            for nh in range(2):
                ib = te.matmul(P1[:, nh * 512:(nh + 1) * 512],
                               ones[bslot(8 + nh)[0]:bslot(8 + nh)[0] + 1, :],
                               bia[bslot(8 + nh)[0]:bslot(8 + nh)[0] + 1,
                                   bslot(8 + nh)[1]:bslot(8 + nh)[1] + 512],
                               start=False, stop=True, skip_group_check=True)
            cinc(ib, "TE")
        te_b2p = cnt["TE"]

        @block.vector
        def _(v):
            wt(v, "DV", "TE", te_b2p)
            wt(v, "DV", "SO", 16 * lci)
            iv = v.tensor_copy(oub[:], P1[:, 0:512])
            cinc(iv, "DV")
            wt(v, "DV", "SP", 16 * lci)
            iv = v.tensor_copy(oub2[:], P1[:, 512:1024])
            cinc(iv, "DV")
        dv_oe = cnt["DV"]

        @block.scalar
        def _(sc):
            wt(sc, "AC", "DV", dv_oe - 1)
            dd = sc.dma_start(
                out=out_d[lseq, lcc * 128:(lcc + 1) * 128, 0:512], in_=oub[:])
            dinc(dd, "SO")

        @block.sync
        def _(sy):
            wt(sy, "SY", "DV", dv_oe)
            dd = sy.dma_start(
                out=out_d[lseq, lcc * 128:(lcc + 1) * 128, 512:1024],
                in_=oub2[:])
            dinc(dd, "SP")

        print(f"[build] sbuf bytes remaining per partition: {nc.sbuf_bytes_remaining}")

        if DBGW:
            @block.gpsimd
            def _(g):
                wt(g, "GP", "SO", 16 * NCHUNKS)
                wt(g, "GP", "SP", 16 * NCHUNKS)
                for dd_, ss_ in [(dw1_d, w1t), (dw2_d, w2s), (dwg_d, wg), (dwo_d, wo)]:
                    dinc(g.dma_start(out=dd_, in_=ss_[:]), "DB")

        @block.sync
        def _(sy):
            wt(sy, "SY", "SO", 16 * NCHUNKS)
            wt(sy, "SY", "SP", 16 * NCHUNKS)
            if DBG or DBGW:
                wt(sy, "SY", "DB", cnt["DB"])

    return nc


_cache = {}


def kernel(**inputs):
    if "nc" not in _cache:
        _cache["consts"] = _make_consts()
        _cache["nc"] = build_program()
    nc, consts = _cache["nc"], _cache["consts"]
    X = np.ascontiguousarray(inputs["L0_attn_X"], dtype=np.float32)
    base = {
        "wq": np.ascontiguousarray(inputs["L0_attn_W_Q"], dtype=np.float32),
        "wk": np.ascontiguousarray(inputs["L0_attn_W_K"], dtype=np.float32),
        "wv": np.ascontiguousarray(inputs["L0_attn_W_V"], dtype=np.float32),
        "wgm": np.ascontiguousarray(inputs["L0_attn_W_G"], dtype=np.float32),
        "wom": np.ascontiguousarray(inputs["L0_attn_W_O"], dtype=np.float32),
        "gnw": np.ascontiguousarray(inputs["L0_attn_gn_weight"], dtype=np.float32),
        "gnb": np.ascontiguousarray(inputs["L0_attn_gn_bias"], dtype=np.float32),
        "w1m": np.ascontiguousarray(inputs["L0_ffn_w1"], dtype=np.float32),
        "b1v": np.ascontiguousarray(inputs["L0_ffn_b1"], dtype=np.float32),
        "w2m": np.ascontiguousarray(inputs["L0_ffn_w2"], dtype=np.float32),
        "b2v": np.ascontiguousarray(inputs["L0_ffn_b2"], dtype=np.float32),
        "tabs": consts["tabs"], "dloc": consts["dloc"], "gqa": consts["gqa"],
        "gk": consts["gk"], "gct": consts["gct"], "ptm": consts["ptm"],
        "idn": consts["idn"], "idn32": consts["idn32"], "ones1": consts["ones1"],
        "kmagic": consts["kmagic"],
    }
    in_maps = []
    for i in range(NCORES):
        m = dict(base)
        m["x"] = np.ascontiguousarray(X[i * BPC:(i + 1) * BPC])
        in_maps.append(m)
    res = run_bass_kernel_spmd(nc, in_maps, core_ids=list(range(NCORES)))
    return np.concatenate([r["out"] for r in res.results], axis=0)


# version 27
# speedup vs baseline: 1.0483x; 1.0483x over previous
import math
import numpy as np
import ml_dtypes
import concourse.bass as bass
import concourse.mybir as mybir
from concourse.bass_utils import run_bass_kernel_spmd

F32 = mybir.dt.float32
BF16 = mybir.dt.bfloat16
AF = mybir.ActivationFunctionType
ALU = mybir.AluOpType
AX = mybir.AxisListType
BF = ml_dtypes.bfloat16

NCORES = 8
B, S, D = 16, 2048, 1024
BPC = B // NCORES
H, d = 8, 128
C = 128
NCH = S // C
NCHUNKS = BPC * NCH
FH = 4096
HG = 4


def _make_consts():
    g = 1.0 - np.exp(np.linspace(math.log(1 / 32.0), math.log(1 / 512.0), H))
    half = d // 2
    scale = (np.arange(0, d, 2, dtype=np.float64) + 0.4 * d) / (1.4 * d)
    pos = np.arange(S, dtype=np.float64)
    sc = scale[None, :] ** (pos[:, None] / 512.0)
    inv_freq = 1.0 / 10000.0 ** (np.arange(half, dtype=np.float64) / half)
    sinus = pos[:, None] * inv_freq[None, :]
    sn, cs = np.sin(sinus), np.cos(sinus)
    rep2 = lambda a: np.repeat(a, 2, axis=1)
    cosQ = rep2(cs * sc).T
    sinQ = rep2(sn * sc).T
    cosK = rep2(cs / sc).T
    sinK = rep2(sn / sc).T
    tabs = np.zeros((NCH, 128, 4, C), dtype=np.float64)
    for cc in range(NCH):
        sl = slice(cc * C, (cc + 1) * C)
        tabs[cc, :, 0, :] = cosQ[:, sl]
        tabs[cc, :, 1, :] = sinQ[:, sl]
        tabs[cc, :, 2, :] = cosK[:, sl]
        tabs[cc, :, 3, :] = sinK[:, sl]
    ar = np.arange(C)
    diff = ar[None, :] - ar[:, None]
    dloc = np.zeros((C, H, C), dtype=np.float64)
    for hh in range(H):
        dloc[:, hh, :] = np.where(diff >= 0, g[hh] ** np.maximum(diff, 0), 0.0)
    gqa = np.broadcast_to(g[None, :, None] ** (ar[None, None, :] - C),
                          (128, H, C)).copy()
    gk = (g[None, :] ** (C - ar)[:, None]).astype(np.float32)
    PT = np.zeros((128, 128), dtype=np.float64)
    for j in range(64):
        PT[2 * j + 1, 2 * j] = -1.0
        PT[2 * j, 2 * j + 1] = 1.0
    gct = np.broadcast_to((g ** C).astype(np.float32)[None, :], (128, H)).copy()
    return {
        "tabs": tabs.astype(BF), "dloc": dloc.astype(BF), "gqa": gqa.astype(BF),
        "gk": gk, "gct": gct, "ptm": PT.astype(BF),
        "idn": np.eye(128, dtype=BF), "idn32": np.eye(128, dtype=np.float32),
        "ones1": np.ones((128, 128), dtype=BF),
        "kmagic": np.full((128, 1), 0x5f3759df, dtype=np.int32),
    }


DBG = False
DBGW = False
PARANOID = False


def build_program():
    from contextlib import ExitStack
    nc = bass.Bass("TRN2", target_bir_lowering=False)
    dt = lambda n, s, ty: nc.dram_tensor(n, s, ty, kind="ExternalInput").ap()
    x_d = dt("x", [BPC, S, D], F32)
    wq_d = dt("wq", [H, d, d], F32)
    wk_d = dt("wk", [H, d, d], F32)
    wv_d = dt("wv", [H, d, d], F32)
    wg_d = dt("wgm", [D, D], F32)
    wo_d = dt("wom", [D, D], F32)
    gnw_d = dt("gnw", [D], F32)
    gnb_d = dt("gnb", [D], F32)
    w1_d = dt("w1m", [FH, D], F32)
    b1_d = dt("b1v", [FH], F32)
    w2_d = dt("w2m", [D, FH], F32)
    b2_d = dt("b2v", [D], F32)
    tabs_d = dt("tabs", [NCH, 128, 4, C], BF16)
    dloc_d = dt("dloc", [C, H, C], BF16)
    gqa_d = dt("gqa", [128, H, C], BF16)
    gk_d = dt("gk", [C, H], F32)
    gct_d = dt("gct", [128, H], F32)
    ptm_d = dt("ptm", [128, 128], BF16)
    idn_d = dt("idn", [128, 128], BF16)
    idn32_d = dt("idn32", [128, 128], F32)
    ones_d = dt("ones1", [128, 128], BF16)
    kc_d = dt("kmagic", [128, 1], mybir.dt.int32)
    out_d = nc.dram_tensor("out", [BPC, S, D], F32, kind="ExternalOutput").ap()
    if DBG:
        dbg_d = nc.dram_tensor("dbg", [16, 128, 1024], BF16, kind="ExternalOutput").ap()
        dbgf_d = nc.dram_tensor("dbgf", [4, 128, 1024], BF16, kind="ExternalOutput").ap()
    if DBGW:
        dw1_d = nc.dram_tensor("dw1", [128, H, FH], BF16, kind="ExternalOutput").ap()
        dw2_d = nc.dram_tensor("dw2", [128, FH // 128, D], BF16, kind="ExternalOutput").ap()
        dwg_d = nc.dram_tensor("dwg", [128, H, D], BF16, kind="ExternalOutput").ap()
        dwo_d = nc.dram_tensor("dwo", [128, H, D], BF16, kind="ExternalOutput").ap()

    sem, cnt, waited = {}, {}, {}

    def dinc(inst, k):
        inst.then_inc(sem[k], 16)
        cnt[k] += 16
        return cnt[k]

    def cinc(inst, k):
        inst.then_inc(sem[k], 1)
        cnt[k] += 1
        return cnt[k]

    def wt(eng, ename, k, v=None):
        if v is None:
            v = cnt[k]
        if PARANOID and ename in ("TE", "DV", "AC"):
            for kk in ("TE", "DV", "AC"):
                if kk != ename and cnt[kk] > 0:
                    key2 = (ename, kk)
                    if waited.get(key2, -1) < cnt[kk]:
                        eng.wait_ge(sem[kk], cnt[kk])
                        waited[key2] = cnt[kk]
        if v <= 0:
            return
        key = (ename, k)
        if waited.get(key, -1) < v:
            eng.wait_ge(sem[k], v)
            waited[key] = v

    with ExitStack() as ctx:
        sb = lambda n, s, ty: ctx.enter_context(nc.sbuf_tensor(n, s, ty))
        w1t = sb("w1t", [128, H, FH], BF16)
        w2s = sb("w2s", [128, FH // 128, D], BF16)
        wg = sb("wg", [128, H, D], BF16)
        wo = sb("wo", [128, H, D], BF16)
        wq = sb("wqs", [128, H, d], BF16)
        wk = sb("wks", [128, H, d], BF16)
        wv = sb("wvs", [128, H, d], BF16)
        dloc = sb("dlocs", [C, H, C], BF16)
        gqa = sb("gqas", [128, H, C], BF16)
        gk = sb("gks", [C, H], F32)
        gct = sb("gcts", [128, H], F32)
        ptm = sb("ptms", [128, 128], BF16)
        idn = sb("idns", [128, 128], BF16)
        idn32 = sb("idn32s", [128, 128], F32)
        ones = sb("oness", [128, 128], BF16)
        kcs = sb("kcss", [128, 1], mybir.dt.int32)
        gnw = sb("gnws", [128, H], F32)
        gnb = sb("gnbs", [128, H], F32)
        bia = sb("biab", [128, 2048], BF16)
        bslot = lambda s_: ((s_ % 3) * 32, (s_ // 3) * 512)
        rbf = sb("rbfs", [128, H, 128], BF16)
        tab = sb("tabb", [128, 4, C], BF16)
        xv = sb("xvb", [128, D], F32)
        st1 = sb("st1b", [128, 4], F32)
        gs = sb("gsb", [128, 32], F32)

        vpl = sb("vpl", [128, 1], F32)
        vhl = sb("vhl", [128, 1], F32)
        rl = sb("rl", [128, 1], F32)
        al = sb("al", [128, 1], F32)
        sil = sb("sil", [128, 1], mybir.dt.int32)
        vpg = sb("vpg", [128, 8], F32)
        vhg = sb("vhg", [128, 8], F32)
        rg = sb("rg", [128, 8], F32)
        ag = sb("ag", [128, 8], F32)
        sig = sb("sig", [128, 8], mybir.dt.int32)
        P0 = ctx.enter_context(nc.psum_tensor("P0", [128, 1024], F32))
        P1 = ctx.enter_context(nc.psum_tensor("P1", [128, 1024], F32))
        P2 = ctx.enter_context(nc.psum_tensor("P2", [128, 1024], F32))
        PTp = ctx.enter_context(nc.psum_tensor("PTp", [128, 1024], BF16))

        for s_name in ["LX", "LT", "LW", "LA", "LB", "LS", "S0", "S1", "SO", "SP", "DB", "TE", "DV", "AC"]:
            sem[s_name] = ctx.enter_context(nc.semaphore(s_name))
            cnt[s_name] = 0

        block = ctx.enter_context(nc.Block())

        P0v = P0[:].rearrange("p (h e) -> p h e", h=H)
        P1v = P1[:].rearrange("p (h e) -> p h e", h=H)
        P2v = P2[:].rearrange("p (h e) -> p h e", h=H)
        PTv = PTp[:].rearrange("p (h e) -> p h e", h=H)
        flat = lambda t: t[:].rearrange("p h e -> p (h e)")
        gvals = 1.0 - np.exp(np.linspace(math.log(1 / 32.0),
                                         math.log(1 / 512.0), H))
        gC = [float(gv_ ** C) for gv_ in gvals]

        I32 = mybir.dt.int32
        lnsum, lnsq = st1[:, 0:1], st1[:, 1:2]
        negmu, e2 = st1[:, 2:3], st1[:, 3:4]
        varp, vh, r_, a_ = vpl[:], vhl[:], rl[:], al[:]
        mu2 = vh
        bp = e2
        shi = sil[:]
        g_negsum, g_sq = gs[:, 0:8], gs[:, 8:16]
        g_negmu, g_e2 = gs[:, 16:24], gs[:, 24:32]
        g_varp, g_vh, g_r, g_a = vpg[:], vhg[:], rg[:], ag[:]
        g_mu2 = g_vh
        g_bp = g_e2
        g_shi = sig[:]

        def gv(v, inst):
            wt(v, "DV", "DV", cinc(inst, "DV"))

        def rsqrt_dve(v, varp_ap, vh_ap, r_ap, a_ap, shi_ap, nsl):
            iv = v.tensor_scalar_mul(vh_ap, varp_ap, -0.5)
            gv(v, iv)
            kb = kcs[:] if nsl == 1 else kcs[:].broadcast_to([128, nsl])
            iv = v.tensor_single_scalar(shi_ap, varp_ap.bitcast(I32),
                                        1, ALU.arith_shift_right)
            gv(v, iv)
            iv = v.tensor_sub(r_ap.bitcast(I32), kb, shi_ap)
            gv(v, iv)
            for _ in range(2):
                iv = v.tensor_mul(a_ap, r_ap, r_ap)
                gv(v, iv)
                iv = v.tensor_tensor(a_ap, a_ap, vh_ap, ALU.mult)
                gv(v, iv)
                iv = v.tensor_scalar_add(a_ap, a_ap, 1.5)
                gv(v, iv)
                iv = v.tensor_mul(r_ap, r_ap, a_ap)
                gv(v, iv)

        @block.sync
        def _(sy):
            dinc(sy.dma_start(out=xv[:], in_=x_d[0, 0:C, :]), "LX")
            dinc(sy.dma_start(out=tab[:], in_=tabs_d[0]), "LT")
            dinc(sy.dma_start(out=gnw[:], in_=gnw_d.rearrange("(h e) -> e h", h=H), allow_slow_non_contiguous=True), "LS")
            dinc(sy.dma_start(out=gnb[:], in_=gnb_d.rearrange("(h e) -> e h", h=H), allow_slow_non_contiguous=True), "LS")
            dinc(sy.dma_start(out=gk[:], in_=gk_d), "LS")
            dinc(sy.dma_start(out=gct[:], in_=gct_d), "LS")
            dinc(sy.dma_start(out=idn32[:], in_=idn32_d), "LS")
            dinc(sy.dma_start(out=kcs[:], in_=kc_d), "LS")
        LS_CONSTS = cnt["LS"]

        @block.gpsimd
        def _(g):
            dinc(g.dma_start(out=idn[:], in_=idn_d), "LW")
        LW_IDN = cnt["LW"]

        @block.gpsimd
        def _(g):
            for dst, src in [
                (dloc[:], dloc_d), (gqa[:], gqa_d),
                (ptm[:], ptm_d), (ones[:], ones_d),
                (wq[:], wq_d.rearrange("h a e -> a h e")),
                (wk[:], wk_d.rearrange("h a e -> a h e")),
                (wv[:], wv_d.rearrange("h a e -> a h e")),
                (wg[:], wg_d.rearrange("(k p) n -> p k n", p=128)),
                (wo[:], wo_d.rearrange("(k p) n -> p k n", p=128)),
            ] + [
                (bia[bslot(s_)[0]:bslot(s_)[0] + 1,
                     bslot(s_)[1]:bslot(s_)[1] + 512],
                 (b1_d if s_ < 8 else b2_d)[(s_ % 8) * 512:(s_ % 8) * 512 + 512]
                 .rearrange("(a n) -> a n", a=1))
                for s_ in range(10)

            ]:
                dinc(g.dma_start(out=dst, in_=src), "LW")
        LW_ALLC = cnt["LW"]

        with ExitStack() as pctx:
            psb = lambda n, s, ty: pctx.enter_context(nc.sbuf_tensor(n, s, ty))
            wstrip = psb("wstrip", [128, 2, 1024], F32)
            wstrip2 = psb("wstrip2", [128, 2, 1024], BF16)
            sgn = psb("sgnb", [128, 1024], BF16)
            s1b = psb("s1bb", [128, 1024], BF16)
            abuf = psb("abufb", [128, 2, 512], F32)
            a1t = psb("a1tb", [128, 1], F32)
            a2s = psb("a2sb", [128, 8], F32)
            a2p = psb("a2pb", [128, 8], F32)

            dv_alpha_base = cnt["DV"]

            def _a2_incs_before(j):
                return j + j // 8

            alpha_cnt = []

            @block.scalar
            def _(sc):
                for i in range(64):
                    if i >= 2:
                        wt(sc, "AC", "DV", dv_alpha_base + _a2_incs_before(i - 2) + 1)
                    sd, pp = divmod(i, 8)
                    dd = sc.dma_start(
                        out=abuf[:, i % 2, :],
                        in_=w2_d[sd * 128:(sd + 1) * 128, pp * 512:(pp + 1) * 512])
                    key = "LA" if i % 2 == 0 else "LB"
                    alpha_cnt.append((key, dinc(dd, key)))

            for i in range(64):
                sd, pp = divmod(i, 8)

                @block.vector
                def _(v, i=i, sd=sd, pp=pp):
                    wt(v, "DV", *alpha_cnt[i])
                    iv = v.reduce_sum(a2p[:, pp:pp + 1], abuf[:, i % 2, :], AX.X,
                                      apply_absolute_value=True)
                    if pp == 7:
                        wt(v, "DV", "DV", cinc(iv, "DV"))
                        iv = v.tensor_reduce(a2s[:, sd:sd + 1], a2p[:], AX.X, ALU.add)
                    cinc(iv, "DV")

            @block.vector
            def _(v):
                wt(v, "DV", "DV")
                iv = v.tensor_scalar_mul(a2s[:], a2s[:], 1.0 / FH)
                cinc(iv, "DV")
            DV_ALPHA2 = cnt["DV"]
            if DBG:
                @block.gpsimd
                def _(g):
                    wt(g, "GP", "DV", DV_ALPHA2)
                    dinc(g.dma_start(out=dbgf_d[3, :, 8:16], in_=a2s[:],
                                     allow_slow_non_contiguous=True), "DB")

            ac_w1_base = cnt["AC"]
            w1_cnt = []

            @block.sync
            def _(sy):
                for s_ in range(32):
                    if s_ >= 2:
                        wt(sy, "SY", "AC", ac_w1_base + (s_ - 1))
                        wt(sy, "SY", "DV", DV_ALPHA2 + 3 * (s_ - 2) + 1)
                    dd = sy.dma_start(out=wstrip[:, s_ % 2, :],
                                      in_=w1_d[s_ * 128:(s_ + 1) * 128, :])
                    key = "S0" if s_ % 2 == 0 else "S1"
                    w1_cnt.append((key, dinc(dd, key)))

            for s_ in range(32):
                @block.vector
                def _(v, s_=s_):
                    wt(v, "DV", *w1_cnt[s_])
                    iv = v.reduce_sum(a1t[:], wstrip[:, s_ % 2, :], AX.X,
                                      apply_absolute_value=True)
                    wt(v, "DV", "DV", cinc(iv, "DV"))
                    iv = v.tensor_scalar_mul(a1t[:], a1t[:], 1.0 / D)
                    cinc(iv, "DV")

                @block.scalar
                def _(sc, s_=s_):
                    wt(sc, "AC", "DV")
                    wt(sc, "AC", "TE")
                    wt(sc, "AC", *w1_cnt[s_])
                    sc.sign(sgn[:], wstrip[:, s_ % 2, :])
                    ia = sc.mul(s1b[:], sgn[:], a1t[:])
                    cinc(ia, "AC")

                @block.tensor
                def _(te, s_=s_):
                    wt(te, "TE", "AC")
                    wt(te, "TE", "LW", LW_ALLC)
                    if s_ > 0:
                        wt(te, "TE", "DV")
                    for kf in range(H):
                        it = te.transpose(PTv[:, kf, :],
                                          s1b[:, kf * 128:(kf + 1) * 128], idn[:])
                    cinc(it, "TE")

                @block.vector
                def _(v, s_=s_):
                    wt(v, "DV", "TE")
                    iv = v.tensor_copy(w1t[:, :, s_ * 128:(s_ + 1) * 128], PTv)
                    cinc(iv, "DV")
            DV_W1 = cnt["DV"]

            ac_w2_base = cnt["AC"]
            w2_cnt = []

            @block.gpsimd
            def _(g):
                for i in range(32):
                    if i >= 2:
                        wt(g, "GP", "AC", ac_w2_base + (i - 1))
                    sd, pc = divmod(i, 4)
                    dd = g.dma_start(
                        out=wstrip2[:, i % 2, :],
                        in_=w2_d[sd * 128:(sd + 1) * 128, pc * 1024:(pc + 1) * 1024])
                    key = "S0" if i % 2 == 0 else "S1"
                    w2_cnt.append((key, dinc(dd, key)))

            for i in range(32):
                sd, pc = divmod(i, 4)

                @block.scalar
                def _(sc, i=i, sd=sd):
                    wt(sc, "AC", *w2_cnt[i])
                    wt(sc, "AC", "DV", DV_ALPHA2)
                    wt(sc, "AC", "TE")
                    sc.sign(sgn[:], wstrip2[:, i % 2, :])
                    ia = sc.mul(s1b[:], sgn[:], a2s[:, sd:sd + 1])
                    cinc(ia, "AC")

                @block.tensor
                def _(te, i=i):
                    wt(te, "TE", "AC")
                    wt(te, "TE", "DV")
                    for j in range(H):
                        it = te.transpose(PTv[:, j, :],
                                          s1b[:, j * 128:(j + 1) * 128], idn[:])
                    cinc(it, "TE")

                @block.vector
                def _(v, i=i, sd=sd, pc=pc):
                    wt(v, "DV", "TE")
                    iv = v.tensor_copy(
                        w2s[:, pc * 8:(pc + 1) * 8, sd * 128:(sd + 1) * 128], PTv)
                    cinc(iv, "DV")
        DV_PRO = cnt["DV"]
        AC_PRO = cnt["AC"]
        TE_PRO = cnt["TE"]
        LS_PRO = cnt["LS"]
        LA_PRO = cnt["LA"]

        oub = sb("oub", [128, 512], F32)
        oub2 = sb("oub2", [128, 512], F32)
        xnf = sb("xnfb", [128, H, C], BF16)
        qbf = sb("qbfb", [128, H, C], BF16)
        kbf = sb("kbfb", [128, H, C], BF16)
        vtm = sb("vtmb", [128, H, d], BF16)
        sA = sb("sAb", [128, H, C], BF16)
        sB = sb("sBb", [128, H, C], BF16)
        sC = sb("sCb", [128, H, C], BF16)
        sD = sb("sDb", [128, H, C], BF16)

        prev = {"imm": 0, "silu": 0, "gelu3": 0, "oevac": 0, "hfm2": DV_PRO,
                "xpk": 0}
        pend = {"dv_xn": 0}

        def ln_section(cj, gate):
            @block.vector
            def _(v, cj=cj, gate=gate):
                if gate:
                    wt(v, "DV", "LW", LW_ALLC)
                    wt(v, "DV", "LS", LS_PRO)
                    wt(v, "DV", "LA", LA_PRO)
                    wt(v, "DV", "TE", TE_PRO)
                    wt(v, "DV", "AC", AC_PRO)
                wt(v, "DV", "LX", 16 * (cj + 1))
                iv = v.reduce_sum(lnsum, xv[:], AX.X, negate=True)
                gv(v, iv)
                iv = v.scalar_tensor_tensor(flat(qbf), xv[:], 1.0, xv[:],
                                            ALU.bypass, ALU.mult, accum_out=lnsq)
                gv(v, iv)
                iv = v.tensor_scalar_mul(negmu, lnsum, 1.0 / D)
                gv(v, iv)
                iv = v.tensor_scalar(e2, lnsq, 1.0 / D, 1e-5, ALU.mult, ALU.add)
                gv(v, iv)
                iv = v.tensor_mul(mu2, negmu, negmu)
                gv(v, iv)
                iv = v.tensor_sub(varp, e2, mu2)
                gv(v, iv)
                rsqrt_dve(v, varp, vh, r_, a_, shi, 1)
                iv = v.tensor_scalar(flat(sD), xv[:], negmu, None, ALU.add)
                gv(v, iv)
                iv = v.tensor_scalar(flat(sD), flat(sD), r_, None, ALU.mult)
                cinc(iv, "DV")
            pend["dv_xn"] = cnt["DV"]

        ln_section(0, True)

        for ci in range(NCHUNKS):
            seq, cc = divmod(ci, NCH)
            first = (cc == 0)

            @block.tensor
            def _(te, ci=ci):
                if ci == 0:
                    wt(te, "TE", "LS", LS_PRO)
                    wt(te, "TE", "AC", AC_PRO)
                wt(te, "TE", "DV", pend["dv_xn"])
                wt(te, "TE", "DV", prev["hfm2"])
                for kf in range(H):
                    it = te.transpose(PTv[:, kf, :], sD[:, kf, :], idn[:])
                cinc(it, "TE")
            te_xnT = cnt["TE"]

            @block.vector
            def _(v):
                wt(v, "DV", "TE", te_xnT)
                iv = v.tensor_copy(flat(xnf), PTp[:])
                cinc(iv, "DV")
            dv_xnf = cnt["DV"]

            @block.tensor
            def _(te):
                wt(te, "TE", "DV", dv_xnf)
                wt(te, "TE", "AC", prev["silu"])
                for h in range(H):
                    iq = te.matmul(P0v[:, h, :], wq[:, h, :], xnf[:, h, :],
                                   start=True, stop=True)
                cinc(iq, "TE")
            te_q = cnt["TE"]

            if ci > 0:
                @block.tensor
                def _(te):
                    wt(te, "TE", "AC", prev["gelu3"])
                    for j in range(H):
                        it = te.transpose(PTv[:, j, :], sA[:, j, :], idn[:])
                    cinc(it, "TE")
                te_hT3p = cnt["TE"]

            @block.tensor
            def _(te):
                wt(te, "TE", "AC", prev["gelu3"])
                for h in range(H):
                    ik = te.matmul(P2v[:, h, :], wk[:, h, :], xnf[:, h, :],
                                   start=True, stop=True)
                cinc(ik, "TE")
            te_k = cnt["TE"]

            @block.vector
            def _(v):
                wt(v, "DV", "TE", te_q)
                iv = v.tensor_copy(flat(qbf), P0[:])
                cinc(iv, "DV")
            dv_qe = cnt["DV"]

            if ci > 0:
                @block.vector
                def _(v):
                    wt(v, "DV", "TE", te_hT3p)
                    iv = v.tensor_copy(flat(sC), PTp[:])
                    cinc(iv, "DV")
                dv_hfm3p = cnt["DV"]

            @block.vector
            def _(v):
                wt(v, "DV", "TE", te_k)
                iv = v.tensor_copy(flat(kbf), P2[:])
                cinc(iv, "DV")
            dv_ke = cnt["DV"]

            @block.tensor
            def _(te):
                wt(te, "TE", "DV", dv_qe)
                for h in range(H):
                    ir = te.matmul(P0v[:, h, :], ptm[:], qbf[:, h, :],
                                   start=True, stop=True)
                cinc(ir, "TE")
                wt(te, "TE", "DV", dv_ke)
                for h in range(H):
                    ir = te.matmul(P2v[:, h, :], ptm[:], kbf[:, h, :],
                                   start=True, stop=True)
                cinc(ir, "TE")
            te_rk = cnt["TE"]
            te_rq = te_rk - 1

            if ci > 0:
                @block.tensor
                def _(te):
                    wt(te, "TE", "DV", dv_hfm3p)
                    for j in range(H):
                        for nh in range(2):
                            iw = te.matmul(P1[:, nh * 512:(nh + 1) * 512],
                                           sC[:, j, :],
                                           w2s[:, 24 + j, nh * 512:(nh + 1) * 512],
                                           start=False, stop=False,
                                           skip_group_check=True)
                    for nh in range(2):
                        ib = te.matmul(
                            P1[:, nh * 512:(nh + 1) * 512],
                            ones[bslot(8 + nh)[0]:bslot(8 + nh)[0] + 1, :],
                            bia[bslot(8 + nh)[0]:bslot(8 + nh)[0] + 1,
                                bslot(8 + nh)[1]:bslot(8 + nh)[1] + 512],
                            start=False, stop=True, skip_group_check=True)
                    cinc(ib, "TE")
                te_b2p = cnt["TE"]

            cosQ = tab[:, 0:1, :].broadcast_to([128, H, C])
            sinQ = tab[:, 1:2, :].broadcast_to([128, H, C])
            cosK = tab[:, 2:3, :].broadcast_to([128, H, C])
            sinK = tab[:, 3:4, :].broadcast_to([128, H, C])

            @block.vector
            def _(v, ci=ci):
                wt(v, "DV", "LT", 16 * (ci + 1))
                wt(v, "DV", "TE", te_rq)
                iv = v.tensor_tensor(sA[:], P0v, sinQ, ALU.mult)
                cinc(iv, "DV")
            dv_rqe = cnt["DV"]

            @block.tensor
            def _(te):
                wt(te, "TE", "DV", dv_rqe)
                for h in range(H):
                    iv_ = te.matmul(P0v[:, h, :], xnf[:, h, :], wv[:, h, :],
                                    start=True, stop=True)
                cinc(iv_, "TE")
            te_v = cnt["TE"]

            @block.vector
            def _(v, ci=ci, first=first):
                v.tensor_mul(sB[:], qbf[:], cosQ)
                iv = v.tensor_add(qbf[:], sA[:], sB[:])
                cinc(iv, "DV")
                wt(v, "DV", "TE", te_rk)
                iv = v.tensor_tensor(sB[:], P2v, sinK, ALU.mult)
                cinc(iv, "DV")
                v.tensor_mul(sA[:], kbf[:], cosK)
                iv = v.tensor_add(kbf[:], sA[:], sB[:])
                prev["xpk"] = cinc(iv, "DV")
                if not first:
                    iv = v.tensor_mul(sD[:], qbf[:], gqa[:])
                    cinc(iv, "DV")
                wt(v, "DV", "TE", te_v)
                iv = v.tensor_copy(flat(vtm), P0[:])
                cinc(iv, "DV")
            dv_kx = prev["xpk"]
            dv_ve = cnt["DV"]

            if ci + 1 < NCHUNKS:
                ncc2 = (ci + 1) % NCH

                @block.gpsimd
                def _(g, ncc2=ncc2):
                    wt(g, "GP", "DV", prev["xpk"])
                    dd = g.dma_start(out=tab[:], in_=tabs_d[ncc2])
                    dinc(dd, "LT")

            @block.tensor
            def _(te):
                wt(te, "TE", "DV", dv_kx)
                for h in range(H):
                    it = te.transpose(PTv[:, h, :], kbf[:, h, :], idn[:])
                cinc(it, "TE")
                for h in range(H):
                    ia_ = te.matmul(P2v[:, h, :], kbf[:, h, :], qbf[:, h, :],
                                    start=True, stop=True)
                cinc(ia_, "TE")
            te_a = cnt["TE"]
            te_ksT = te_a - 1

            @block.vector
            def _(v):
                wt(v, "DV", "TE", te_a)
                iv = v.tensor_tensor(sA[:], P2v, dloc[:], ALU.mult)
                cinc(iv, "DV")
                wt(v, "DV", "TE", te_ksT)
                gkb = gk[:].unsqueeze(2).broadcast_to([128, H, 128])
                iv = v.tensor_tensor(sC[:], PTv, gkb, ALU.mult)
                cinc(iv, "DV")
            dv_ks = cnt["DV"]
            dv_ad = dv_ks - 1

            if ci > 0:
                @block.vector
                def _(v, ci=ci):
                    wt(v, "DV", "TE", te_b2p)
                    wt(v, "DV", "SO", 16 * (ci - 1))
                    iv = v.tensor_copy(oub[:], P1[:, 0:512])
                    cinc(iv, "DV")
                    wt(v, "DV", "SP", 16 * (ci - 1))
                    iv = v.tensor_copy(oub2[:], P1[:, 512:1024])
                    prev["oevac"] = cinc(iv, "DV")
                dv_oe = prev["oevac"]
                pseq, pcc = divmod(ci - 1, NCH)

                @block.scalar
                def _(sc, pseq=pseq, pcc=pcc, dv_oe=dv_oe):
                    wt(sc, "AC", "DV", dv_oe - 1)
                    dd = sc.dma_start(
                        out=out_d[pseq, pcc * 128:(pcc + 1) * 128, 0:512],
                        in_=oub[:])
                    dinc(dd, "SO")

                @block.sync
                def _(sy, pseq=pseq, pcc=pcc, dv_oe=dv_oe):
                    wt(sy, "SY", "DV", dv_oe)
                    dd = sy.dma_start(
                        out=out_d[pseq, pcc * 128:(pcc + 1) * 128, 512:1024],
                        in_=oub2[:])
                    dinc(dd, "SP")

            @block.tensor
            def _(te, first=first):
                wt(te, "TE", "DV", dv_ad)
                wt(te, "TE", "DV", dv_ve)
                for h in range(H):
                    iy = te.matmul(P0v[:, h, :], sA[:, h, :], vtm[:, h, :],
                                   start=True, stop=first)
                    if not first:
                        iy = te.matmul(P0v[:, h, :], sD[:, h, :], rbf[:, h, :],
                                       start=False, stop=True)
                cinc(iy, "TE")
                wt(te, "TE", "DV", dv_ks)
                for h in range(H):
                    is_ = te.matmul(P2v[:, h, :], sC[:, h, :], vtm[:, h, :],
                                    start=True, stop=first)
                    if not first:
                        is_ = te.matmul(P2v[:, h, :], idn[:], rbf[:, h, :],
                                        start=False, stop=True)
                cinc(is_, "TE")
            te_s = cnt["TE"]
            te_y = te_s - 1

            @block.tensor
            def _(te):
                wt(te, "TE", "DV", prev["oevac"])
                im = te.matmul(P1[:, 0:512], idn32[:], xv[:, 0:512],
                               start=True, stop=False, skip_group_check=True)
                im = te.matmul(P1[:, 512:1024], idn32[:], xv[:, 512:1024],
                               start=True, stop=False, skip_group_check=True)
                prev["imm"] = cinc(im, "TE")

            if ci + 1 < NCHUNKS:
                nseq, ncc = divmod(ci + 1, NCH)

                @block.gpsimd
                def _(g, nseq=nseq, ncc=ncc):
                    wt(g, "GP", "TE", prev["imm"])
                    dd = g.dma_start(out=xv[:],
                                     in_=x_d[nseq, ncc * 128:(ncc + 1) * 128, :])
                    dinc(dd, "LX")

            @block.vector
            def _(v):
                wt(v, "DV", "TE", te_s)
                gcb = gct[:].unsqueeze(2).broadcast_to([128, H, 128])
                iv = v.tensor_tensor(rbf[:], P2v, gcb, ALU.mult)
                cinc(iv, "DV")
            dv_rbf = cnt["DV"]

            @block.vector
            def _(v):
                iv = v.tensor_copy(flat(qbf), P0[:])
                gv(v, iv)
                prev["p0f"] = cnt["DV"]
                qbv = qbf[:]
                iv = v.reduce_sum(g_negsum, qbv, AX.X, negate=True)
                gv(v, iv)
                iv = v.tensor_tensor(flat(kbf), qbf[:].rearrange("p h e -> p (h e)"),
                                     flat(qbf), ALU.mult)
                gv(v, iv)
                iv = v.tensor_reduce(g_sq, kbf[:], AX.X, ALU.add)
                gv(v, iv)
                iv = v.tensor_scalar_mul(g_negmu, g_negsum, 1.0 / 128.0)
                gv(v, iv)
                iv = v.tensor_scalar(g_e2, g_sq, 1.0 / 128.0, 1e-5, ALU.mult, ALU.add)
                gv(v, iv)
                iv = v.tensor_mul(g_mu2, g_negmu, g_negmu)
                gv(v, iv)
                iv = v.tensor_sub(g_varp, g_e2, g_mu2)
                gv(v, iv)
                rsqrt_dve(v, g_varp, g_vh, g_r, g_a, g_shi, 8)
                nmb = gs[:, 16:24].unsqueeze(2).broadcast_to([128, H, 128])
                grb = rg[:].unsqueeze(2).broadcast_to([128, H, 128])
                iv = v.scalar_tensor_tensor(qbf[:], qbf[:], 1.0, nmb,
                                            ALU.bypass, ALU.add)
                gv(v, iv)
                iv = v.tensor_tensor(sB[:], qbf[:], grb, ALU.mult)
                cinc(iv, "DV")
            dv_ynp = cnt["DV"]

            @block.tensor
            def _(te, ci=ci):
                wt(te, "TE", "DV", dv_rbf)
                if ci == 0:
                    wt(te, "TE", "DV", DV_PRO)
                for kf in range(H):
                    for nh in range(2):
                        ih = te.matmul(P2[:, nh * 512:(nh + 1) * 512],
                                       xnf[:, kf, :],
                                       w1t[:, kf, nh * 512:nh * 512 + 512],
                                       start=(kf == 0), stop=False)
                for nh in range(2):
                    ih = te.matmul(P2[:, nh * 512:(nh + 1) * 512],
                                   ones[bslot(nh)[0]:bslot(nh)[0] + 1, :],
                                   bia[bslot(nh)[0]:bslot(nh)[0] + 1,
                                       bslot(nh)[1]:bslot(nh)[1] + 512],
                                   start=False, stop=True)
                cinc(ih, "TE")
            te_h = [cnt["TE"], 0, 0, 0]

            @block.scalar
            def _(sc, ci=ci):
                if ci == 0:
                    wt(sc, "AC", "LS", LS_PRO)
                    wt(sc, "AC", "LA", LA_PRO)
                    wt(sc, "AC", "DV", DV_PRO)
                wt(sc, "AC", "TE", te_h[0])
                ia = sc.activation(flat(sA), P2[:], AF.Gelu)
                cinc(ia, "AC")
            ac_gel = [cnt["AC"], 0, 0, 0]

            @block.tensor
            def _(te):
                wt(te, "TE", "DV", prev["p0f"])
                for kf in range(H):
                    for nh in range(2):
                        ig = te.matmul(P0[:, nh * 512:(nh + 1) * 512],
                                       xnf[:, kf, :],
                                       wg[:, kf, nh * 512:(nh + 1) * 512],
                                       start=(kf == 0), stop=(kf == H - 1))
                cinc(ig, "TE")
            te_g = cnt["TE"]

            @block.scalar
            def _(sc):
                wt(sc, "AC", "TE", te_g)
                ia = sc.activation(flat(vtm), P0[:], AF.Silu)
                prev["silu"] = cinc(ia, "AC")
            ac_silu = cnt["AC"]

            @block.tensor
            def _(te):
                wt(te, "TE", "AC", ac_gel[0])
                wt(te, "TE", "DV", dv_ks)
                for j in range(H):
                    it = te.transpose(PTv[:, j, :], sA[:, j, :], idn[:])
                cinc(it, "TE")
                for kf in range(H):
                    for nh in range(2):
                        ih = te.matmul(P2[:, nh * 512:(nh + 1) * 512],
                                       xnf[:, kf, :],
                                       w1t[:, kf, 1024 + nh * 512:1024 + nh * 512 + 512],
                                       start=(kf == 0), stop=False)
                for nh in range(2):
                    ih = te.matmul(P2[:, nh * 512:(nh + 1) * 512],
                                   ones[bslot(2 + nh)[0]:bslot(2 + nh)[0] + 1, :],
                                   bia[bslot(2 + nh)[0]:bslot(2 + nh)[0] + 1,
                                       bslot(2 + nh)[1]:bslot(2 + nh)[1] + 512],
                                   start=False, stop=True)
                cinc(ih, "TE")
            te_h[1] = cnt["TE"]
            te_hT0 = te_h[1] - 1

            @block.vector
            def _(v):
                wt(v, "DV", "TE", te_hT0)
                iv = v.tensor_copy(flat(sC), PTp[:])
                cinc(iv, "DV")
            dv_hfm = [cnt["DV"], 0, 0, 0]

            @block.scalar
            def _(sc, ci=ci):
                wt(sc, "AC", "TE", te_h[1])
                ia = sc.activation(flat(sA), P2[:], AF.Gelu)
                cinc(ia, "AC")
            ac_gel[1] = cnt["AC"]

            @block.tensor
            def _(te):
                wt(te, "TE", "DV", dv_ynp)
                wt(te, "TE", "DV", dv_hfm[0])
                for h in range(H):
                    it = te.transpose(PTv[:, h, :], sB[:, h, :], idn[:])
                cinc(it, "TE")
            te_ynT = cnt["TE"]

            @block.vector
            def _(v):
                wt(v, "DV", "TE", te_ynT)
                gwb = gnw[:].unsqueeze(2).broadcast_to([128, H, 128])
                gbb = gnb[:].unsqueeze(2).broadcast_to([128, H, 128])
                iv = v.scalar_tensor_tensor(kbf[:], PTv, 1.0, gwb, ALU.bypass, ALU.mult)
                gv(v, iv)
                iv = v.tensor_tensor(kbf[:], kbf[:], gbb, ALU.add)
                cinc(iv, "DV")
            dv_ynf = cnt["DV"]

            @block.tensor
            def _(te):
                wt(te, "TE", "DV", dv_hfm[0])
                for j in range(H):
                    for nh in range(2):
                        iw = te.matmul(P1[:, nh * 512:(nh + 1) * 512],
                                       sC[:, j, :],
                                       w2s[:, j, nh * 512:(nh + 1) * 512],
                                       start=False, stop=False,
                                       skip_group_check=True)
                cinc(iw, "TE")
                wt(te, "TE", "AC", ac_silu)
                wt(te, "TE", "DV", dv_ynf)
                for h in range(H):
                    it = te.transpose(PTv[:, h, :], vtm[:, h, :], idn[:])
                cinc(it, "TE")
            te_sgT = cnt["TE"]
            te_w2_0 = te_sgT - 1

            @block.vector
            def _(v):
                wt(v, "DV", "TE", te_sgT)
                iv = v.tensor_tensor(sB[:], PTv, kbf[:], ALU.mult)
                cinc(iv, "DV")
            dv_prod = cnt["DV"]

            if ci + 1 < NCHUNKS:
                ln_section(ci + 1, False)

            @block.tensor
            def _(te):
                wt(te, "TE", "DV", dv_prod)
                for kf in range(H):
                    for nh in range(2):
                        io = te.matmul(P1[:, nh * 512:(nh + 1) * 512],
                                       sB[:, kf, :],
                                       wo[:, kf, nh * 512:(nh + 1) * 512],
                                       start=False, stop=False,
                                       skip_group_check=True)
                cinc(io, "TE")
            te_wo = cnt["TE"]

            for hg in (2, 3):
                @block.tensor
                def _(te, hg=hg):
                    wt(te, "TE", "AC", ac_gel[hg - 1])
                    wt(te, "TE", "DV", dv_prod if hg == 2 else dv_hfm[1])
                    for j in range(H):
                        it = te.transpose(PTv[:, j, :], sA[:, j, :], idn[:])
                    cinc(it, "TE")
                    for kf in range(H):
                        for nh in range(2):
                            ih = te.matmul(P2[:, nh * 512:(nh + 1) * 512],
                                           xnf[:, kf, :],
                                           w1t[:, kf, hg * 1024 + nh * 512:
                                               hg * 1024 + nh * 512 + 512],
                                           start=(kf == 0), stop=False)
                    for nh in range(2):
                        ih = te.matmul(P2[:, nh * 512:(nh + 1) * 512],
                                       ones[bslot(2 * hg + nh)[0]:
                                            bslot(2 * hg + nh)[0] + 1, :],
                                       bia[bslot(2 * hg + nh)[0]:
                                           bslot(2 * hg + nh)[0] + 1,
                                           bslot(2 * hg + nh)[1]:
                                           bslot(2 * hg + nh)[1] + 512],
                                       start=False, stop=True)
                    cinc(ih, "TE")
                te_h[hg] = cnt["TE"]
                te_hTprev = te_h[hg] - 1

                @block.vector
                def _(v, hg=hg, te_hTprev=te_hTprev):
                    wt(v, "DV", "TE", te_hTprev)
                    iv = v.tensor_copy(flat(sC), PTp[:])
                    cinc(iv, "DV")
                dv_hfm[hg - 1] = cnt["DV"]

                @block.scalar
                def _(sc, hg=hg):
                    wt(sc, "AC", "TE", te_h[hg])
                    ia = sc.activation(flat(sA), P2[:], AF.Gelu)
                    cinc(ia, "AC")
                ac_gel[hg] = cnt["AC"]

                @block.tensor
                def _(te, hg=hg):
                    wt(te, "TE", "DV", dv_hfm[hg - 1])
                    for j in range(H):
                        for nh in range(2):
                            iw = te.matmul(P1[:, nh * 512:(nh + 1) * 512],
                                           sC[:, j, :],
                                           w2s[:, (hg - 1) * 8 + j,
                                               nh * 512:(nh + 1) * 512],
                                           start=False, stop=False,
                                           skip_group_check=True)
                    cinc(iw, "TE")
            prev["gelu3"] = ac_gel[3]
            prev["hfm2"] = dv_hfm[2]

        lci = NCHUNKS - 1
        lseq, lcc = divmod(lci, NCH)

        @block.tensor
        def _(te):
            wt(te, "TE", "AC", prev["gelu3"])
            wt(te, "TE", "DV", prev["hfm2"])
            for j in range(H):
                it = te.transpose(PTv[:, j, :], sA[:, j, :], idn[:])
            cinc(it, "TE")
        te_hT3p = cnt["TE"]

        @block.vector
        def _(v):
            wt(v, "DV", "TE", te_hT3p)
            iv = v.tensor_copy(flat(sC), PTp[:])
            cinc(iv, "DV")
        dv_hfm3p = cnt["DV"]

        @block.tensor
        def _(te):
            wt(te, "TE", "DV", dv_hfm3p)
            for j in range(H):
                for nh in range(2):
                    iw = te.matmul(P1[:, nh * 512:(nh + 1) * 512],
                                   sC[:, j, :],
                                   w2s[:, 24 + j, nh * 512:(nh + 1) * 512],
                                   start=False, stop=False,
                                   skip_group_check=True)
            for nh in range(2):
                ib = te.matmul(P1[:, nh * 512:(nh + 1) * 512],
                               ones[bslot(8 + nh)[0]:bslot(8 + nh)[0] + 1, :],
                               bia[bslot(8 + nh)[0]:bslot(8 + nh)[0] + 1,
                                   bslot(8 + nh)[1]:bslot(8 + nh)[1] + 512],
                               start=False, stop=True, skip_group_check=True)
            cinc(ib, "TE")
        te_b2p = cnt["TE"]

        @block.vector
        def _(v):
            wt(v, "DV", "TE", te_b2p)
            wt(v, "DV", "SO", 16 * lci)
            iv = v.tensor_copy(oub[:], P1[:, 0:512])
            cinc(iv, "DV")
            wt(v, "DV", "SP", 16 * lci)
            iv = v.tensor_copy(oub2[:], P1[:, 512:1024])
            cinc(iv, "DV")
        dv_oe = cnt["DV"]

        @block.scalar
        def _(sc):
            wt(sc, "AC", "DV", dv_oe - 1)
            dd = sc.dma_start(
                out=out_d[lseq, lcc * 128:(lcc + 1) * 128, 0:512], in_=oub[:])
            dinc(dd, "SO")

        @block.sync
        def _(sy):
            wt(sy, "SY", "DV", dv_oe)
            dd = sy.dma_start(
                out=out_d[lseq, lcc * 128:(lcc + 1) * 128, 512:1024],
                in_=oub2[:])
            dinc(dd, "SP")

        print(f"[build] sbuf bytes remaining per partition: {nc.sbuf_bytes_remaining}")

        if DBGW:
            @block.gpsimd
            def _(g):
                wt(g, "GP", "SO", 16 * NCHUNKS)
                wt(g, "GP", "SP", 16 * NCHUNKS)
                for dd_, ss_ in [(dw1_d, w1t), (dw2_d, w2s), (dwg_d, wg), (dwo_d, wo)]:
                    dinc(g.dma_start(out=dd_, in_=ss_[:]), "DB")

        @block.sync
        def _(sy):
            wt(sy, "SY", "SO", 16 * NCHUNKS)
            wt(sy, "SY", "SP", 16 * NCHUNKS)
            if DBG or DBGW:
                wt(sy, "SY", "DB", cnt["DB"])

    return nc


_cache = {}


def kernel(**inputs):
    if "nc" not in _cache:
        _cache["consts"] = _make_consts()
        _cache["nc"] = build_program()
    nc, consts = _cache["nc"], _cache["consts"]
    X = np.ascontiguousarray(inputs["L0_attn_X"], dtype=np.float32)
    base = {
        "wq": np.ascontiguousarray(inputs["L0_attn_W_Q"], dtype=np.float32),
        "wk": np.ascontiguousarray(inputs["L0_attn_W_K"], dtype=np.float32),
        "wv": np.ascontiguousarray(inputs["L0_attn_W_V"], dtype=np.float32),
        "wgm": np.ascontiguousarray(inputs["L0_attn_W_G"], dtype=np.float32),
        "wom": np.ascontiguousarray(inputs["L0_attn_W_O"], dtype=np.float32),
        "gnw": np.ascontiguousarray(inputs["L0_attn_gn_weight"], dtype=np.float32),
        "gnb": np.ascontiguousarray(inputs["L0_attn_gn_bias"], dtype=np.float32),
        "w1m": np.ascontiguousarray(inputs["L0_ffn_w1"], dtype=np.float32),
        "b1v": np.ascontiguousarray(inputs["L0_ffn_b1"], dtype=np.float32),
        "w2m": np.ascontiguousarray(inputs["L0_ffn_w2"], dtype=np.float32),
        "b2v": np.ascontiguousarray(inputs["L0_ffn_b2"], dtype=np.float32),
        "tabs": consts["tabs"], "dloc": consts["dloc"], "gqa": consts["gqa"],
        "gk": consts["gk"], "gct": consts["gct"], "ptm": consts["ptm"],
        "idn": consts["idn"], "idn32": consts["idn32"], "ones1": consts["ones1"],
        "kmagic": consts["kmagic"],
    }
    in_maps = []
    for i in range(NCORES):
        m = dict(base)
        m["x"] = np.ascontiguousarray(X[i * BPC:(i + 1) * BPC])
        in_maps.append(m)
    res = run_bass_kernel_spmd(nc, in_maps, core_ids=list(range(NCORES)))
    return np.concatenate([r["out"] for r in res.results], axis=0)


# version 28
# speedup vs baseline: 1.0483x; 1.0000x over previous
import math
import numpy as np
import ml_dtypes
import concourse.bass as bass
import concourse.mybir as mybir
from concourse.bass_utils import run_bass_kernel_spmd

F32 = mybir.dt.float32
BF16 = mybir.dt.bfloat16
AF = mybir.ActivationFunctionType
ALU = mybir.AluOpType
AX = mybir.AxisListType
BF = ml_dtypes.bfloat16

NCORES = 8
B, S, D = 16, 2048, 1024
BPC = B // NCORES
H, d = 8, 128
C = 128
NCH = S // C
NCHUNKS = BPC * NCH
FH = 4096
HG = 4


def _make_consts():
    g = 1.0 - np.exp(np.linspace(math.log(1 / 32.0), math.log(1 / 512.0), H))
    half = d // 2
    scale = (np.arange(0, d, 2, dtype=np.float64) + 0.4 * d) / (1.4 * d)
    pos = np.arange(S, dtype=np.float64)
    sc = scale[None, :] ** (pos[:, None] / 512.0)
    inv_freq = 1.0 / 10000.0 ** (np.arange(half, dtype=np.float64) / half)
    sinus = pos[:, None] * inv_freq[None, :]
    sn, cs = np.sin(sinus), np.cos(sinus)
    rep2 = lambda a: np.repeat(a, 2, axis=1)
    cosQ = rep2(cs * sc).T
    sinQ = rep2(sn * sc).T
    cosK = rep2(cs / sc).T
    sinK = rep2(sn / sc).T
    tabs = np.zeros((NCH, 128, 4, C), dtype=np.float64)
    for cc in range(NCH):
        sl = slice(cc * C, (cc + 1) * C)
        tabs[cc, :, 0, :] = cosQ[:, sl]
        tabs[cc, :, 1, :] = sinQ[:, sl]
        tabs[cc, :, 2, :] = cosK[:, sl]
        tabs[cc, :, 3, :] = sinK[:, sl]
    ar = np.arange(C)
    diff = ar[None, :] - ar[:, None]
    dloc = np.zeros((C, H, C), dtype=np.float64)
    for hh in range(H):
        dloc[:, hh, :] = np.where(diff >= 0, g[hh] ** np.maximum(diff, 0), 0.0)
    gqa = np.broadcast_to(g[None, :, None] ** (ar[None, None, :] - C),
                          (128, H, C)).copy()
    gk = (g[None, :] ** (C - ar)[:, None]).astype(np.float32)
    PT = np.zeros((128, 128), dtype=np.float64)
    for j in range(64):
        PT[2 * j + 1, 2 * j] = -1.0
        PT[2 * j, 2 * j + 1] = 1.0
    gct = np.broadcast_to((g ** C).astype(np.float32)[None, :], (128, H)).copy()
    return {
        "tabs": tabs.astype(BF), "dloc": dloc.astype(BF), "gqa": gqa.astype(BF),
        "gk": gk, "gct": gct, "ptm": PT.astype(BF),
        "idn": np.eye(128, dtype=BF), "idn32": np.eye(128, dtype=np.float32),
        "ones1": np.ones((128, 128), dtype=BF),
        "kmagic": np.full((128, 1), 0x5f3759df, dtype=np.int32),
    }


DBG = False
DBGW = False
PARANOID = False


def build_program():
    from contextlib import ExitStack
    nc = bass.Bass("TRN2", target_bir_lowering=False)
    dt = lambda n, s, ty: nc.dram_tensor(n, s, ty, kind="ExternalInput").ap()
    x_d = dt("x", [BPC, S, D], F32)
    wq_d = dt("wq", [H, d, d], F32)
    wk_d = dt("wk", [H, d, d], F32)
    wv_d = dt("wv", [H, d, d], F32)
    wg_d = dt("wgm", [D, D], F32)
    wo_d = dt("wom", [D, D], F32)
    gnw_d = dt("gnw", [D], F32)
    gnb_d = dt("gnb", [D], F32)
    w1_d = dt("w1m", [FH, D], F32)
    b1_d = dt("b1v", [FH], F32)
    w2_d = dt("w2m", [D, FH], F32)
    b2_d = dt("b2v", [D], F32)
    tabs_d = dt("tabs", [NCH, 128, 4, C], BF16)
    dloc_d = dt("dloc", [C, H, C], BF16)
    gqa_d = dt("gqa", [128, H, C], BF16)
    gk_d = dt("gk", [C, H], F32)
    gct_d = dt("gct", [128, H], F32)
    ptm_d = dt("ptm", [128, 128], BF16)
    idn_d = dt("idn", [128, 128], BF16)
    idn32_d = dt("idn32", [128, 128], F32)
    ones_d = dt("ones1", [128, 128], BF16)
    kc_d = dt("kmagic", [128, 1], mybir.dt.int32)
    out_d = nc.dram_tensor("out", [BPC, S, D], F32, kind="ExternalOutput").ap()
    if DBG:
        dbg_d = nc.dram_tensor("dbg", [16, 128, 1024], BF16, kind="ExternalOutput").ap()
        dbgf_d = nc.dram_tensor("dbgf", [4, 128, 1024], BF16, kind="ExternalOutput").ap()
    if DBGW:
        dw1_d = nc.dram_tensor("dw1", [128, H, FH], BF16, kind="ExternalOutput").ap()
        dw2_d = nc.dram_tensor("dw2", [128, FH // 128, D], BF16, kind="ExternalOutput").ap()
        dwg_d = nc.dram_tensor("dwg", [128, H, D], BF16, kind="ExternalOutput").ap()
        dwo_d = nc.dram_tensor("dwo", [128, H, D], BF16, kind="ExternalOutput").ap()

    sem, cnt, waited = {}, {}, {}

    def dinc(inst, k):
        inst.then_inc(sem[k], 16)
        cnt[k] += 16
        return cnt[k]

    def cinc(inst, k):
        inst.then_inc(sem[k], 1)
        cnt[k] += 1
        return cnt[k]

    def wt(eng, ename, k, v=None):
        if v is None:
            v = cnt[k]
        if PARANOID and ename in ("TE", "DV", "AC"):
            for kk in ("TE", "DV", "AC"):
                if kk != ename and cnt[kk] > 0:
                    key2 = (ename, kk)
                    if waited.get(key2, -1) < cnt[kk]:
                        eng.wait_ge(sem[kk], cnt[kk])
                        waited[key2] = cnt[kk]
        if v <= 0:
            return
        key = (ename, k)
        if waited.get(key, -1) < v:
            eng.wait_ge(sem[k], v)
            waited[key] = v

    with ExitStack() as ctx:
        sb = lambda n, s, ty: ctx.enter_context(nc.sbuf_tensor(n, s, ty))
        w1t = sb("w1t", [128, H, FH], BF16)
        w2s = sb("w2s", [128, FH // 128, D], BF16)
        wg = sb("wg", [128, H, D], BF16)
        wo = sb("wo", [128, H, D], BF16)
        wq = sb("wqs", [128, H, d], BF16)
        wk = sb("wks", [128, H, d], BF16)
        wv = sb("wvs", [128, H, d], BF16)
        dloc = sb("dlocs", [C, H, C], BF16)
        gqa = sb("gqas", [128, H, C], BF16)
        gk = sb("gks", [C, H], F32)
        gct = sb("gcts", [128, H], F32)
        ptm = sb("ptms", [128, 128], BF16)
        idn = sb("idns", [128, 128], BF16)
        idn32 = sb("idn32s", [128, 128], F32)
        ones = sb("oness", [128, 128], BF16)
        kcs = sb("kcss", [128, 1], mybir.dt.int32)
        gnw = sb("gnws", [128, H], F32)
        gnb = sb("gnbs", [128, H], F32)
        bia = sb("biab", [128, 2048], BF16)
        bslot = lambda s_: ((s_ % 3) * 32, (s_ // 3) * 512)
        rbf = sb("rbfs", [128, H, 128], BF16)
        tab = sb("tabb", [128, 4, C], BF16)
        xv = sb("xvb", [128, D], F32)
        st1 = sb("st1b", [128, 4], F32)
        gs = sb("gsb", [128, 32], F32)

        vpl = sb("vpl", [128, 1], F32)
        vhl = sb("vhl", [128, 1], F32)
        rl = sb("rl", [128, 1], F32)
        al = sb("al", [128, 1], F32)
        sil = sb("sil", [128, 1], mybir.dt.int32)
        vpg = sb("vpg", [128, 8], F32)
        vhg = sb("vhg", [128, 8], F32)
        rg = sb("rg", [128, 8], F32)
        ag = sb("ag", [128, 8], F32)
        sig = sb("sig", [128, 8], mybir.dt.int32)
        P0 = ctx.enter_context(nc.psum_tensor("P0", [128, 1024], F32))
        P1 = ctx.enter_context(nc.psum_tensor("P1", [128, 1024], F32))
        P2 = ctx.enter_context(nc.psum_tensor("P2", [128, 1024], F32))
        PTp = ctx.enter_context(nc.psum_tensor("PTp", [128, 1024], BF16))

        for s_name in ["LX", "LT", "LW", "LA", "LB", "LS", "S0", "S1", "SO", "SP", "DB", "TE", "DV", "AC"]:
            sem[s_name] = ctx.enter_context(nc.semaphore(s_name))
            cnt[s_name] = 0

        block = ctx.enter_context(nc.Block())

        P0v = P0[:].rearrange("p (h e) -> p h e", h=H)
        P1v = P1[:].rearrange("p (h e) -> p h e", h=H)
        P2v = P2[:].rearrange("p (h e) -> p h e", h=H)
        PTv = PTp[:].rearrange("p (h e) -> p h e", h=H)
        flat = lambda t: t[:].rearrange("p h e -> p (h e)")
        gvals = 1.0 - np.exp(np.linspace(math.log(1 / 32.0),
                                         math.log(1 / 512.0), H))
        gC = [float(gv_ ** C) for gv_ in gvals]

        I32 = mybir.dt.int32
        lnsum, lnsq = st1[:, 0:1], st1[:, 1:2]
        negmu, e2 = st1[:, 2:3], st1[:, 3:4]
        varp, vh, r_, a_ = vpl[:], vhl[:], rl[:], al[:]
        mu2 = vh
        bp = e2
        shi = sil[:]
        g_negsum, g_sq = gs[:, 0:8], gs[:, 8:16]
        g_negmu, g_e2 = gs[:, 16:24], gs[:, 24:32]
        g_varp, g_vh, g_r, g_a = vpg[:], vhg[:], rg[:], ag[:]
        g_mu2 = g_vh
        g_bp = g_e2
        g_shi = sig[:]

        def gv(v, inst):
            wt(v, "DV", "DV", cinc(inst, "DV"))

        def rsqrt_dve(v, varp_ap, vh_ap, r_ap, a_ap, shi_ap, nsl):
            iv = v.tensor_scalar_mul(vh_ap, varp_ap, -0.5)
            gv(v, iv)
            kb = kcs[:] if nsl == 1 else kcs[:].broadcast_to([128, nsl])
            iv = v.tensor_single_scalar(shi_ap, varp_ap.bitcast(I32),
                                        1, ALU.arith_shift_right)
            gv(v, iv)
            iv = v.tensor_sub(r_ap.bitcast(I32), kb, shi_ap)
            gv(v, iv)
            for _ in range(2):
                iv = v.tensor_mul(a_ap, r_ap, r_ap)
                gv(v, iv)
                iv = v.tensor_tensor(a_ap, a_ap, vh_ap, ALU.mult)
                gv(v, iv)
                iv = v.tensor_scalar_add(a_ap, a_ap, 1.5)
                gv(v, iv)
                iv = v.tensor_mul(r_ap, r_ap, a_ap)
                gv(v, iv)

        @block.sync
        def _(sy):
            dinc(sy.dma_start(out=xv[:], in_=x_d[0, 0:C, :]), "LX")
            dinc(sy.dma_start(out=tab[:], in_=tabs_d[0]), "LT")
            dinc(sy.dma_start(out=gnw[:], in_=gnw_d.rearrange("(h e) -> e h", h=H), allow_slow_non_contiguous=True), "LS")
            dinc(sy.dma_start(out=gnb[:], in_=gnb_d.rearrange("(h e) -> e h", h=H), allow_slow_non_contiguous=True), "LS")
            dinc(sy.dma_start(out=gk[:], in_=gk_d), "LS")
            dinc(sy.dma_start(out=gct[:], in_=gct_d), "LS")
            dinc(sy.dma_start(out=idn32[:], in_=idn32_d), "LS")
            dinc(sy.dma_start(out=kcs[:], in_=kc_d), "LS")
        LS_CONSTS = cnt["LS"]

        @block.gpsimd
        def _(g):
            dinc(g.dma_start(out=idn[:], in_=idn_d), "LW")
        LW_IDN = cnt["LW"]

        @block.gpsimd
        def _(g):
            for dst, src in [
                (dloc[:], dloc_d), (gqa[:], gqa_d),
                (ptm[:], ptm_d), (ones[:], ones_d),
                (wq[:], wq_d.rearrange("h a e -> a h e")),
                (wk[:], wk_d.rearrange("h a e -> a h e")),
                (wv[:], wv_d.rearrange("h a e -> a h e")),
                (wg[:], wg_d.rearrange("(k p) n -> p k n", p=128)),
                (wo[:], wo_d.rearrange("(k p) n -> p k n", p=128)),
            ] + [
                (bia[bslot(s_)[0]:bslot(s_)[0] + 1,
                     bslot(s_)[1]:bslot(s_)[1] + 512],
                 (b1_d if s_ < 8 else b2_d)[(s_ % 8) * 512:(s_ % 8) * 512 + 512]
                 .rearrange("(a n) -> a n", a=1))
                for s_ in range(10)

            ]:
                dinc(g.dma_start(out=dst, in_=src), "LW")
        LW_ALLC = cnt["LW"]

        with ExitStack() as pctx:
            psb = lambda n, s, ty: pctx.enter_context(nc.sbuf_tensor(n, s, ty))
            wstrip = psb("wstrip", [128, 2, 1024], F32)
            wstrip2 = psb("wstrip2", [128, 2, 1024], BF16)
            sgn = psb("sgnb", [128, 1024], BF16)
            s1b = psb("s1bb", [128, 1024], BF16)
            abuf = psb("abufb", [128, 2, 512], F32)
            a1t = psb("a1tb", [128, 1], F32)
            a2s = psb("a2sb", [128, 8], F32)
            a2p = psb("a2pb", [128, 8], F32)

            dv_alpha_base = cnt["DV"]

            def _a2_incs_before(j):
                return j + j // 8

            alpha_cnt = []

            @block.scalar
            def _(sc):
                for i in range(64):
                    if i >= 2:
                        wt(sc, "AC", "DV", dv_alpha_base + _a2_incs_before(i - 2) + 1)
                    sd, pp = divmod(i, 8)
                    dd = sc.dma_start(
                        out=abuf[:, i % 2, :],
                        in_=w2_d[sd * 128:(sd + 1) * 128, pp * 512:(pp + 1) * 512])
                    key = "LA" if i % 2 == 0 else "LB"
                    alpha_cnt.append((key, dinc(dd, key)))

            for i in range(64):
                sd, pp = divmod(i, 8)

                @block.vector
                def _(v, i=i, sd=sd, pp=pp):
                    wt(v, "DV", *alpha_cnt[i])
                    iv = v.reduce_sum(a2p[:, pp:pp + 1], abuf[:, i % 2, :], AX.X,
                                      apply_absolute_value=True)
                    if pp == 7:
                        wt(v, "DV", "DV", cinc(iv, "DV"))
                        iv = v.tensor_reduce(a2s[:, sd:sd + 1], a2p[:], AX.X, ALU.add)
                    cinc(iv, "DV")

            @block.vector
            def _(v):
                wt(v, "DV", "DV")
                iv = v.tensor_scalar_mul(a2s[:], a2s[:], 1.0 / FH)
                cinc(iv, "DV")
            DV_ALPHA2 = cnt["DV"]
            if DBG:
                @block.gpsimd
                def _(g):
                    wt(g, "GP", "DV", DV_ALPHA2)
                    dinc(g.dma_start(out=dbgf_d[3, :, 8:16], in_=a2s[:],
                                     allow_slow_non_contiguous=True), "DB")

            ac_w1_base = cnt["AC"]
            w1_cnt = []

            @block.sync
            def _(sy):
                for s_ in range(32):
                    if s_ >= 2:
                        wt(sy, "SY", "AC", ac_w1_base + (s_ - 1))
                        wt(sy, "SY", "DV", DV_ALPHA2 + 3 * (s_ - 2) + 1)
                    dd = sy.dma_start(out=wstrip[:, s_ % 2, :],
                                      in_=w1_d[s_ * 128:(s_ + 1) * 128, :])
                    key = "S0" if s_ % 2 == 0 else "S1"
                    w1_cnt.append((key, dinc(dd, key)))

            for s_ in range(32):
                @block.vector
                def _(v, s_=s_):
                    wt(v, "DV", *w1_cnt[s_])
                    iv = v.reduce_sum(a1t[:], wstrip[:, s_ % 2, :], AX.X,
                                      apply_absolute_value=True)
                    wt(v, "DV", "DV", cinc(iv, "DV"))
                    iv = v.tensor_scalar_mul(a1t[:], a1t[:], 1.0 / D)
                    cinc(iv, "DV")

                @block.scalar
                def _(sc, s_=s_):
                    wt(sc, "AC", "DV")
                    wt(sc, "AC", "TE")
                    wt(sc, "AC", *w1_cnt[s_])
                    sc.sign(sgn[:], wstrip[:, s_ % 2, :])
                    ia = sc.mul(s1b[:], sgn[:], a1t[:])
                    cinc(ia, "AC")

                @block.tensor
                def _(te, s_=s_):
                    wt(te, "TE", "AC")
                    wt(te, "TE", "LW", LW_ALLC)
                    if s_ > 0:
                        wt(te, "TE", "DV")
                    for kf in range(H):
                        it = te.transpose(PTv[:, kf, :],
                                          s1b[:, kf * 128:(kf + 1) * 128], idn[:])
                    cinc(it, "TE")

                @block.vector
                def _(v, s_=s_):
                    wt(v, "DV", "TE")
                    iv = v.tensor_copy(w1t[:, :, s_ * 128:(s_ + 1) * 128], PTv)
                    cinc(iv, "DV")
            DV_W1 = cnt["DV"]

            ac_w2_base = cnt["AC"]
            w2_cnt = []

            @block.gpsimd
            def _(g):
                for i in range(32):
                    if i >= 2:
                        wt(g, "GP", "AC", ac_w2_base + (i - 1))
                    sd, pc = divmod(i, 4)
                    dd = g.dma_start(
                        out=wstrip2[:, i % 2, :],
                        in_=w2_d[sd * 128:(sd + 1) * 128, pc * 1024:(pc + 1) * 1024])
                    key = "S0" if i % 2 == 0 else "S1"
                    w2_cnt.append((key, dinc(dd, key)))

            for i in range(32):
                sd, pc = divmod(i, 4)

                @block.scalar
                def _(sc, i=i, sd=sd):
                    wt(sc, "AC", *w2_cnt[i])
                    wt(sc, "AC", "DV", DV_ALPHA2)
                    wt(sc, "AC", "TE")
                    sc.sign(sgn[:], wstrip2[:, i % 2, :])
                    ia = sc.mul(s1b[:], sgn[:], a2s[:, sd:sd + 1])
                    cinc(ia, "AC")

                @block.tensor
                def _(te, i=i):
                    wt(te, "TE", "AC")
                    wt(te, "TE", "DV")
                    for j in range(H):
                        it = te.transpose(PTv[:, j, :],
                                          s1b[:, j * 128:(j + 1) * 128], idn[:])
                    cinc(it, "TE")

                @block.vector
                def _(v, i=i, sd=sd, pc=pc):
                    wt(v, "DV", "TE")
                    iv = v.tensor_copy(
                        w2s[:, pc * 8:(pc + 1) * 8, sd * 128:(sd + 1) * 128], PTv)
                    cinc(iv, "DV")
        DV_PRO = cnt["DV"]
        AC_PRO = cnt["AC"]
        TE_PRO = cnt["TE"]
        LS_PRO = cnt["LS"]
        LA_PRO = cnt["LA"]

        oub = sb("oub", [128, 512], F32)
        oub2 = sb("oub2", [128, 512], F32)
        xnf = sb("xnfb", [128, H, C], BF16)
        qbf = sb("qbfb", [128, H, C], BF16)
        kbf = sb("kbfb", [128, H, C], BF16)
        vtm = sb("vtmb", [128, H, d], BF16)
        sA = sb("sAb", [128, H, C], BF16)
        sB = sb("sBb", [128, H, C], BF16)
        sC = sb("sCb", [128, H, C], BF16)
        sD = sb("sDb", [128, H, C], BF16)

        prev = {"imm": 0, "silu": 0, "gelu3": 0, "oevac": 0, "hfm2": DV_PRO,
                "xpk": 0}
        pend = {"dv_xn": 0}

        def ln_section(cj, gate):
            @block.vector
            def _(v, cj=cj, gate=gate):
                if gate:
                    wt(v, "DV", "LW", LW_ALLC)
                    wt(v, "DV", "LS", LS_PRO)
                    wt(v, "DV", "LA", LA_PRO)
                    wt(v, "DV", "TE", TE_PRO)
                    wt(v, "DV", "AC", AC_PRO)
                wt(v, "DV", "LX", 16 * (cj + 1))
                iv = v.reduce_sum(lnsum, xv[:], AX.X, negate=True)
                gv(v, iv)
                iv = v.scalar_tensor_tensor(flat(qbf), xv[:], 1.0, xv[:],
                                            ALU.bypass, ALU.mult, accum_out=lnsq)
                gv(v, iv)
                iv = v.tensor_scalar_mul(negmu, lnsum, 1.0 / D)
                gv(v, iv)
                iv = v.tensor_scalar(e2, lnsq, 1.0 / D, 1e-5, ALU.mult, ALU.add)
                gv(v, iv)
                iv = v.tensor_mul(mu2, negmu, negmu)
                gv(v, iv)
                iv = v.tensor_sub(varp, e2, mu2)
                gv(v, iv)
                rsqrt_dve(v, varp, vh, r_, a_, shi, 1)
                iv = v.tensor_scalar(flat(sD), xv[:], negmu, None, ALU.add)
                gv(v, iv)
                iv = v.tensor_scalar(flat(sD), flat(sD), r_, None, ALU.mult)
                cinc(iv, "DV")
            pend["dv_xn"] = cnt["DV"]

        ln_section(0, True)

        for ci in range(NCHUNKS):
            seq, cc = divmod(ci, NCH)
            first = (cc == 0)

            @block.tensor
            def _(te, ci=ci):
                if ci == 0:
                    wt(te, "TE", "LS", LS_PRO)
                    wt(te, "TE", "AC", AC_PRO)
                wt(te, "TE", "DV", pend["dv_xn"])
                wt(te, "TE", "DV", prev["hfm2"])
                for kf in range(H):
                    it = te.transpose(PTv[:, kf, :], sD[:, kf, :], idn[:])
                cinc(it, "TE")
            te_xnT = cnt["TE"]

            @block.vector
            def _(v):
                wt(v, "DV", "TE", te_xnT)
                iv = v.tensor_copy(flat(xnf), PTp[:])
                cinc(iv, "DV")
            dv_xnf = cnt["DV"]

            @block.tensor
            def _(te):
                wt(te, "TE", "DV", dv_xnf)
                wt(te, "TE", "AC", prev["silu"])
                for h in range(H):
                    iq = te.matmul(P0v[:, h, :], wq[:, h, :], xnf[:, h, :],
                                   start=True, stop=True)
                cinc(iq, "TE")
            te_q = cnt["TE"]

            if ci > 0:
                @block.tensor
                def _(te):
                    wt(te, "TE", "AC", prev["gelu3"])
                    for j in range(H):
                        it = te.transpose(PTv[:, j, :], sA[:, j, :], idn[:])
                    cinc(it, "TE")
                te_hT3p = cnt["TE"]

            @block.tensor
            def _(te):
                wt(te, "TE", "AC", prev["gelu3"])
                for h in range(H):
                    ik = te.matmul(P2v[:, h, :], wk[:, h, :], xnf[:, h, :],
                                   start=True, stop=True)
                cinc(ik, "TE")
            te_k = cnt["TE"]

            @block.vector
            def _(v):
                wt(v, "DV", "TE", te_q)
                iv = v.tensor_copy(flat(qbf), P0[:])
                cinc(iv, "DV")
            dv_qe = cnt["DV"]

            if ci > 0:
                @block.vector
                def _(v):
                    wt(v, "DV", "TE", te_hT3p)
                    iv = v.tensor_copy(flat(sC), PTp[:])
                    cinc(iv, "DV")
                dv_hfm3p = cnt["DV"]

            @block.vector
            def _(v):
                wt(v, "DV", "TE", te_k)
                iv = v.tensor_copy(flat(kbf), P2[:])
                cinc(iv, "DV")
            dv_ke = cnt["DV"]

            @block.tensor
            def _(te):
                wt(te, "TE", "DV", dv_qe)
                for h in range(H):
                    ir = te.matmul(P0v[:, h, :], ptm[:], qbf[:, h, :],
                                   start=True, stop=True)
                cinc(ir, "TE")
                wt(te, "TE", "DV", dv_ke)
                for h in range(H):
                    ir = te.matmul(P2v[:, h, :], ptm[:], kbf[:, h, :],
                                   start=True, stop=True)
                cinc(ir, "TE")
            te_rk = cnt["TE"]
            te_rq = te_rk - 1

            if ci > 0:
                @block.tensor
                def _(te):
                    wt(te, "TE", "DV", dv_hfm3p)
                    for j in range(H):
                        for nh in range(2):
                            iw = te.matmul(P1[:, nh * 512:(nh + 1) * 512],
                                           sC[:, j, :],
                                           w2s[:, 24 + j, nh * 512:(nh + 1) * 512],
                                           start=False, stop=False,
                                           skip_group_check=True)
                    for nh in range(2):
                        ib = te.matmul(
                            P1[:, nh * 512:(nh + 1) * 512],
                            ones[bslot(8 + nh)[0]:bslot(8 + nh)[0] + 1, :],
                            bia[bslot(8 + nh)[0]:bslot(8 + nh)[0] + 1,
                                bslot(8 + nh)[1]:bslot(8 + nh)[1] + 512],
                            start=False, stop=True, skip_group_check=True)
                    cinc(ib, "TE")
                te_b2p = cnt["TE"]

            cosQ = tab[:, 0:1, :].broadcast_to([128, H, C])
            sinQ = tab[:, 1:2, :].broadcast_to([128, H, C])
            cosK = tab[:, 2:3, :].broadcast_to([128, H, C])
            sinK = tab[:, 3:4, :].broadcast_to([128, H, C])

            @block.vector
            def _(v, ci=ci):
                wt(v, "DV", "LT", 16 * (ci + 1))
                wt(v, "DV", "TE", te_rq)
                iv = v.tensor_tensor(sA[:], P0v, sinQ, ALU.mult)
                cinc(iv, "DV")
            dv_rqe = cnt["DV"]

            @block.tensor
            def _(te):
                wt(te, "TE", "DV", dv_rqe)
                for h in range(H):
                    iv_ = te.matmul(P0v[:, h, :], xnf[:, h, :], wv[:, h, :],
                                    start=True, stop=True)
                cinc(iv_, "TE")
            te_v = cnt["TE"]

            @block.vector
            def _(v, ci=ci, first=first):
                v.tensor_mul(sB[:], qbf[:], cosQ)
                iv = v.tensor_add(qbf[:], sA[:], sB[:])
                cinc(iv, "DV")
                wt(v, "DV", "TE", te_rk)
                iv = v.tensor_tensor(sB[:], P2v, sinK, ALU.mult)
                cinc(iv, "DV")
                v.tensor_mul(sA[:], kbf[:], cosK)
                iv = v.tensor_add(kbf[:], sA[:], sB[:])
                prev["xpk"] = cinc(iv, "DV")
                if not first:
                    iv = v.tensor_mul(sD[:], qbf[:], gqa[:])
                    cinc(iv, "DV")
                wt(v, "DV", "TE", te_v)
                iv = v.tensor_copy(flat(vtm), P0[:])
                cinc(iv, "DV")
            dv_kx = prev["xpk"]
            dv_ve = cnt["DV"]

            if ci + 1 < NCHUNKS:
                ncc2 = (ci + 1) % NCH

                @block.gpsimd
                def _(g, ncc2=ncc2):
                    wt(g, "GP", "DV", prev["xpk"])
                    dd = g.dma_start(out=tab[:], in_=tabs_d[ncc2])
                    dinc(dd, "LT")

            @block.tensor
            def _(te):
                wt(te, "TE", "DV", dv_kx)
                for h in range(H):
                    it = te.transpose(PTv[:, h, :], kbf[:, h, :], idn[:])
                cinc(it, "TE")
                for h in range(H):
                    ia_ = te.matmul(P2v[:, h, :], kbf[:, h, :], qbf[:, h, :],
                                    start=True, stop=True)
                cinc(ia_, "TE")
            te_a = cnt["TE"]
            te_ksT = te_a - 1

            @block.vector
            def _(v):
                wt(v, "DV", "TE", te_a)
                iv = v.tensor_tensor(sA[:], P2v, dloc[:], ALU.mult)
                cinc(iv, "DV")
                wt(v, "DV", "TE", te_ksT)
                gkb = gk[:].unsqueeze(2).broadcast_to([128, H, 128])
                iv = v.tensor_tensor(sC[:], PTv, gkb, ALU.mult)
                cinc(iv, "DV")
            dv_ks = cnt["DV"]
            dv_ad = dv_ks - 1

            if ci > 0:
                @block.vector
                def _(v, ci=ci):
                    wt(v, "DV", "TE", te_b2p)
                    wt(v, "DV", "SO", 16 * (ci - 1))
                    iv = v.tensor_copy(oub[:], P1[:, 0:512])
                    cinc(iv, "DV")
                    wt(v, "DV", "SP", 16 * (ci - 1))
                    iv = v.tensor_copy(oub2[:], P1[:, 512:1024])
                    prev["oevac"] = cinc(iv, "DV")
                dv_oe = prev["oevac"]
                pseq, pcc = divmod(ci - 1, NCH)

                @block.scalar
                def _(sc, pseq=pseq, pcc=pcc, dv_oe=dv_oe):
                    wt(sc, "AC", "DV", dv_oe - 1)
                    dd = sc.dma_start(
                        out=out_d[pseq, pcc * 128:(pcc + 1) * 128, 0:512],
                        in_=oub[:])
                    dinc(dd, "SO")

                @block.sync
                def _(sy, pseq=pseq, pcc=pcc, dv_oe=dv_oe):
                    wt(sy, "SY", "DV", dv_oe)
                    dd = sy.dma_start(
                        out=out_d[pseq, pcc * 128:(pcc + 1) * 128, 512:1024],
                        in_=oub2[:])
                    dinc(dd, "SP")

            @block.tensor
            def _(te, first=first):
                wt(te, "TE", "DV", dv_ad)
                wt(te, "TE", "DV", dv_ve)
                for h in range(H):
                    iy = te.matmul(P0v[:, h, :], sA[:, h, :], vtm[:, h, :],
                                   start=True, stop=first)
                    if not first:
                        iy = te.matmul(P0v[:, h, :], sD[:, h, :], rbf[:, h, :],
                                       start=False, stop=True)
                cinc(iy, "TE")
                wt(te, "TE", "DV", dv_ks)
                for h in range(H):
                    is_ = te.matmul(P2v[:, h, :], sC[:, h, :], vtm[:, h, :],
                                    start=True, stop=first)
                    if not first:
                        is_ = te.matmul(P2v[:, h, :], idn[:], rbf[:, h, :],
                                        start=False, stop=True)
                cinc(is_, "TE")
            te_s = cnt["TE"]
            te_y = te_s - 1

            @block.tensor
            def _(te):
                wt(te, "TE", "DV", prev["oevac"])
                im = te.matmul(P1[:, 0:512], idn32[:], xv[:, 0:512],
                               start=True, stop=False, skip_group_check=True)
                im = te.matmul(P1[:, 512:1024], idn32[:], xv[:, 512:1024],
                               start=True, stop=False, skip_group_check=True)
                prev["imm"] = cinc(im, "TE")

            if ci + 1 < NCHUNKS:
                nseq, ncc = divmod(ci + 1, NCH)

                @block.sync
                def _(sy, nseq=nseq, ncc=ncc):
                    wt(sy, "SY", "TE", prev["imm"])
                    dd = sy.dma_start(out=xv[:],
                                      in_=x_d[nseq, ncc * 128:(ncc + 1) * 128, :])
                    dinc(dd, "LX")

            @block.vector
            def _(v):
                wt(v, "DV", "TE", te_s)
                gcb = gct[:].unsqueeze(2).broadcast_to([128, H, 128])
                iv = v.tensor_tensor(rbf[:], P2v, gcb, ALU.mult)
                cinc(iv, "DV")
            dv_rbf = cnt["DV"]

            @block.vector
            def _(v):
                iv = v.tensor_copy(flat(qbf), P0[:])
                gv(v, iv)
                prev["p0f"] = cnt["DV"]
                qbv = qbf[:]
                iv = v.reduce_sum(g_negsum, qbv, AX.X, negate=True)
                gv(v, iv)
                iv = v.tensor_tensor(flat(kbf), qbf[:].rearrange("p h e -> p (h e)"),
                                     flat(qbf), ALU.mult)
                gv(v, iv)
                iv = v.tensor_reduce(g_sq, kbf[:], AX.X, ALU.add)
                gv(v, iv)
                iv = v.tensor_scalar_mul(g_negmu, g_negsum, 1.0 / 128.0)
                gv(v, iv)
                iv = v.tensor_scalar(g_e2, g_sq, 1.0 / 128.0, 1e-5, ALU.mult, ALU.add)
                gv(v, iv)
                iv = v.tensor_mul(g_mu2, g_negmu, g_negmu)
                gv(v, iv)
                iv = v.tensor_sub(g_varp, g_e2, g_mu2)
                gv(v, iv)
                rsqrt_dve(v, g_varp, g_vh, g_r, g_a, g_shi, 8)
                nmb = gs[:, 16:24].unsqueeze(2).broadcast_to([128, H, 128])
                grb = rg[:].unsqueeze(2).broadcast_to([128, H, 128])
                iv = v.scalar_tensor_tensor(qbf[:], qbf[:], 1.0, nmb,
                                            ALU.bypass, ALU.add)
                gv(v, iv)
                iv = v.tensor_tensor(sB[:], qbf[:], grb, ALU.mult)
                cinc(iv, "DV")
            dv_ynp = cnt["DV"]

            @block.tensor
            def _(te, ci=ci):
                wt(te, "TE", "DV", dv_rbf)
                if ci == 0:
                    wt(te, "TE", "DV", DV_PRO)
                for kf in range(H):
                    for nh in range(2):
                        ih = te.matmul(P2[:, nh * 512:(nh + 1) * 512],
                                       xnf[:, kf, :],
                                       w1t[:, kf, nh * 512:nh * 512 + 512],
                                       start=(kf == 0), stop=False)
                for nh in range(2):
                    ih = te.matmul(P2[:, nh * 512:(nh + 1) * 512],
                                   ones[bslot(nh)[0]:bslot(nh)[0] + 1, :],
                                   bia[bslot(nh)[0]:bslot(nh)[0] + 1,
                                       bslot(nh)[1]:bslot(nh)[1] + 512],
                                   start=False, stop=True)
                cinc(ih, "TE")
            te_h = [cnt["TE"], 0, 0, 0]

            @block.scalar
            def _(sc, ci=ci):
                if ci == 0:
                    wt(sc, "AC", "LS", LS_PRO)
                    wt(sc, "AC", "LA", LA_PRO)
                    wt(sc, "AC", "DV", DV_PRO)
                wt(sc, "AC", "TE", te_h[0])
                ia = sc.activation(flat(sA), P2[:], AF.Gelu)
                cinc(ia, "AC")
            ac_gel = [cnt["AC"], 0, 0, 0]

            @block.tensor
            def _(te):
                wt(te, "TE", "DV", prev["p0f"])
                for kf in range(H):
                    for nh in range(2):
                        ig = te.matmul(P0[:, nh * 512:(nh + 1) * 512],
                                       xnf[:, kf, :],
                                       wg[:, kf, nh * 512:(nh + 1) * 512],
                                       start=(kf == 0), stop=(kf == H - 1))
                cinc(ig, "TE")
            te_g = cnt["TE"]

            @block.scalar
            def _(sc):
                wt(sc, "AC", "TE", te_g)
                ia = sc.activation(flat(vtm), P0[:], AF.Silu)
                prev["silu"] = cinc(ia, "AC")
            ac_silu = cnt["AC"]

            @block.tensor
            def _(te):
                wt(te, "TE", "AC", ac_gel[0])
                wt(te, "TE", "DV", dv_ks)
                for j in range(H):
                    it = te.transpose(PTv[:, j, :], sA[:, j, :], idn[:])
                cinc(it, "TE")
                for kf in range(H):
                    for nh in range(2):
                        ih = te.matmul(P2[:, nh * 512:(nh + 1) * 512],
                                       xnf[:, kf, :],
                                       w1t[:, kf, 1024 + nh * 512:1024 + nh * 512 + 512],
                                       start=(kf == 0), stop=False)
                for nh in range(2):
                    ih = te.matmul(P2[:, nh * 512:(nh + 1) * 512],
                                   ones[bslot(2 + nh)[0]:bslot(2 + nh)[0] + 1, :],
                                   bia[bslot(2 + nh)[0]:bslot(2 + nh)[0] + 1,
                                       bslot(2 + nh)[1]:bslot(2 + nh)[1] + 512],
                                   start=False, stop=True)
                cinc(ih, "TE")
            te_h[1] = cnt["TE"]
            te_hT0 = te_h[1] - 1

            @block.vector
            def _(v):
                wt(v, "DV", "TE", te_hT0)
                iv = v.tensor_copy(flat(sC), PTp[:])
                cinc(iv, "DV")
            dv_hfm = [cnt["DV"], 0, 0, 0]

            @block.scalar
            def _(sc, ci=ci):
                wt(sc, "AC", "TE", te_h[1])
                ia = sc.activation(flat(sA), P2[:], AF.Gelu)
                cinc(ia, "AC")
            ac_gel[1] = cnt["AC"]

            @block.tensor
            def _(te):
                wt(te, "TE", "DV", dv_ynp)
                wt(te, "TE", "DV", dv_hfm[0])
                for h in range(H):
                    it = te.transpose(PTv[:, h, :], sB[:, h, :], idn[:])
                cinc(it, "TE")
            te_ynT = cnt["TE"]

            @block.vector
            def _(v):
                wt(v, "DV", "TE", te_ynT)
                gwb = gnw[:].unsqueeze(2).broadcast_to([128, H, 128])
                gbb = gnb[:].unsqueeze(2).broadcast_to([128, H, 128])
                iv = v.scalar_tensor_tensor(kbf[:], PTv, 1.0, gwb, ALU.bypass, ALU.mult)
                gv(v, iv)
                iv = v.tensor_tensor(kbf[:], kbf[:], gbb, ALU.add)
                cinc(iv, "DV")
            dv_ynf = cnt["DV"]

            @block.tensor
            def _(te):
                wt(te, "TE", "DV", dv_hfm[0])
                for j in range(H):
                    for nh in range(2):
                        iw = te.matmul(P1[:, nh * 512:(nh + 1) * 512],
                                       sC[:, j, :],
                                       w2s[:, j, nh * 512:(nh + 1) * 512],
                                       start=False, stop=False,
                                       skip_group_check=True)
                cinc(iw, "TE")
                wt(te, "TE", "AC", ac_silu)
                wt(te, "TE", "DV", dv_ynf)
                for h in range(H):
                    it = te.transpose(PTv[:, h, :], vtm[:, h, :], idn[:])
                cinc(it, "TE")
            te_sgT = cnt["TE"]
            te_w2_0 = te_sgT - 1

            @block.vector
            def _(v):
                wt(v, "DV", "TE", te_sgT)
                iv = v.tensor_tensor(sB[:], PTv, kbf[:], ALU.mult)
                cinc(iv, "DV")
            dv_prod = cnt["DV"]

            if ci + 1 < NCHUNKS:
                ln_section(ci + 1, False)

            @block.tensor
            def _(te):
                wt(te, "TE", "DV", dv_prod)
                for kf in range(H):
                    for nh in range(2):
                        io = te.matmul(P1[:, nh * 512:(nh + 1) * 512],
                                       sB[:, kf, :],
                                       wo[:, kf, nh * 512:(nh + 1) * 512],
                                       start=False, stop=False,
                                       skip_group_check=True)
                cinc(io, "TE")
            te_wo = cnt["TE"]

            for hg in (2, 3):
                @block.tensor
                def _(te, hg=hg):
                    wt(te, "TE", "AC", ac_gel[hg - 1])
                    wt(te, "TE", "DV", dv_prod if hg == 2 else dv_hfm[1])
                    for j in range(H):
                        it = te.transpose(PTv[:, j, :], sA[:, j, :], idn[:])
                    cinc(it, "TE")
                    for kf in range(H):
                        for nh in range(2):
                            ih = te.matmul(P2[:, nh * 512:(nh + 1) * 512],
                                           xnf[:, kf, :],
                                           w1t[:, kf, hg * 1024 + nh * 512:
                                               hg * 1024 + nh * 512 + 512],
                                           start=(kf == 0), stop=False)
                    for nh in range(2):
                        ih = te.matmul(P2[:, nh * 512:(nh + 1) * 512],
                                       ones[bslot(2 * hg + nh)[0]:
                                            bslot(2 * hg + nh)[0] + 1, :],
                                       bia[bslot(2 * hg + nh)[0]:
                                           bslot(2 * hg + nh)[0] + 1,
                                           bslot(2 * hg + nh)[1]:
                                           bslot(2 * hg + nh)[1] + 512],
                                       start=False, stop=True)
                    cinc(ih, "TE")
                te_h[hg] = cnt["TE"]
                te_hTprev = te_h[hg] - 1

                @block.vector
                def _(v, hg=hg, te_hTprev=te_hTprev):
                    wt(v, "DV", "TE", te_hTprev)
                    iv = v.tensor_copy(flat(sC), PTp[:])
                    cinc(iv, "DV")
                dv_hfm[hg - 1] = cnt["DV"]

                @block.scalar
                def _(sc, hg=hg):
                    wt(sc, "AC", "TE", te_h[hg])
                    ia = sc.activation(flat(sA), P2[:], AF.Gelu)
                    cinc(ia, "AC")
                ac_gel[hg] = cnt["AC"]

                @block.tensor
                def _(te, hg=hg):
                    wt(te, "TE", "DV", dv_hfm[hg - 1])
                    for j in range(H):
                        for nh in range(2):
                            iw = te.matmul(P1[:, nh * 512:(nh + 1) * 512],
                                           sC[:, j, :],
                                           w2s[:, (hg - 1) * 8 + j,
                                               nh * 512:(nh + 1) * 512],
                                           start=False, stop=False,
                                           skip_group_check=True)
                    cinc(iw, "TE")
            prev["gelu3"] = ac_gel[3]
            prev["hfm2"] = dv_hfm[2]

        lci = NCHUNKS - 1
        lseq, lcc = divmod(lci, NCH)

        @block.tensor
        def _(te):
            wt(te, "TE", "AC", prev["gelu3"])
            wt(te, "TE", "DV", prev["hfm2"])
            for j in range(H):
                it = te.transpose(PTv[:, j, :], sA[:, j, :], idn[:])
            cinc(it, "TE")
        te_hT3p = cnt["TE"]

        @block.vector
        def _(v):
            wt(v, "DV", "TE", te_hT3p)
            iv = v.tensor_copy(flat(sC), PTp[:])
            cinc(iv, "DV")
        dv_hfm3p = cnt["DV"]

        @block.tensor
        def _(te):
            wt(te, "TE", "DV", dv_hfm3p)
            for j in range(H):
                for nh in range(2):
                    iw = te.matmul(P1[:, nh * 512:(nh + 1) * 512],
                                   sC[:, j, :],
                                   w2s[:, 24 + j, nh * 512:(nh + 1) * 512],
                                   start=False, stop=False,
                                   skip_group_check=True)
            for nh in range(2):
                ib = te.matmul(P1[:, nh * 512:(nh + 1) * 512],
                               ones[bslot(8 + nh)[0]:bslot(8 + nh)[0] + 1, :],
                               bia[bslot(8 + nh)[0]:bslot(8 + nh)[0] + 1,
                                   bslot(8 + nh)[1]:bslot(8 + nh)[1] + 512],
                               start=False, stop=True, skip_group_check=True)
            cinc(ib, "TE")
        te_b2p = cnt["TE"]

        @block.vector
        def _(v):
            wt(v, "DV", "TE", te_b2p)
            wt(v, "DV", "SO", 16 * lci)
            iv = v.tensor_copy(oub[:], P1[:, 0:512])
            cinc(iv, "DV")
            wt(v, "DV", "SP", 16 * lci)
            iv = v.tensor_copy(oub2[:], P1[:, 512:1024])
            cinc(iv, "DV")
        dv_oe = cnt["DV"]

        @block.scalar
        def _(sc):
            wt(sc, "AC", "DV", dv_oe - 1)
            dd = sc.dma_start(
                out=out_d[lseq, lcc * 128:(lcc + 1) * 128, 0:512], in_=oub[:])
            dinc(dd, "SO")

        @block.sync
        def _(sy):
            wt(sy, "SY", "DV", dv_oe)
            dd = sy.dma_start(
                out=out_d[lseq, lcc * 128:(lcc + 1) * 128, 512:1024],
                in_=oub2[:])
            dinc(dd, "SP")

        print(f"[build] sbuf bytes remaining per partition: {nc.sbuf_bytes_remaining}")

        if DBGW:
            @block.gpsimd
            def _(g):
                wt(g, "GP", "SO", 16 * NCHUNKS)
                wt(g, "GP", "SP", 16 * NCHUNKS)
                for dd_, ss_ in [(dw1_d, w1t), (dw2_d, w2s), (dwg_d, wg), (dwo_d, wo)]:
                    dinc(g.dma_start(out=dd_, in_=ss_[:]), "DB")

        @block.sync
        def _(sy):
            wt(sy, "SY", "SO", 16 * NCHUNKS)
            wt(sy, "SY", "SP", 16 * NCHUNKS)
            if DBG or DBGW:
                wt(sy, "SY", "DB", cnt["DB"])

    return nc


_cache = {}


def kernel(**inputs):
    if "nc" not in _cache:
        _cache["consts"] = _make_consts()
        _cache["nc"] = build_program()
    nc, consts = _cache["nc"], _cache["consts"]
    X = np.ascontiguousarray(inputs["L0_attn_X"], dtype=np.float32)
    base = {
        "wq": np.ascontiguousarray(inputs["L0_attn_W_Q"], dtype=np.float32),
        "wk": np.ascontiguousarray(inputs["L0_attn_W_K"], dtype=np.float32),
        "wv": np.ascontiguousarray(inputs["L0_attn_W_V"], dtype=np.float32),
        "wgm": np.ascontiguousarray(inputs["L0_attn_W_G"], dtype=np.float32),
        "wom": np.ascontiguousarray(inputs["L0_attn_W_O"], dtype=np.float32),
        "gnw": np.ascontiguousarray(inputs["L0_attn_gn_weight"], dtype=np.float32),
        "gnb": np.ascontiguousarray(inputs["L0_attn_gn_bias"], dtype=np.float32),
        "w1m": np.ascontiguousarray(inputs["L0_ffn_w1"], dtype=np.float32),
        "b1v": np.ascontiguousarray(inputs["L0_ffn_b1"], dtype=np.float32),
        "w2m": np.ascontiguousarray(inputs["L0_ffn_w2"], dtype=np.float32),
        "b2v": np.ascontiguousarray(inputs["L0_ffn_b2"], dtype=np.float32),
        "tabs": consts["tabs"], "dloc": consts["dloc"], "gqa": consts["gqa"],
        "gk": consts["gk"], "gct": consts["gct"], "ptm": consts["ptm"],
        "idn": consts["idn"], "idn32": consts["idn32"], "ones1": consts["ones1"],
        "kmagic": consts["kmagic"],
    }
    in_maps = []
    for i in range(NCORES):
        m = dict(base)
        m["x"] = np.ascontiguousarray(X[i * BPC:(i + 1) * BPC])
        in_maps.append(m)
    res = run_bass_kernel_spmd(nc, in_maps, core_ids=list(range(NCORES)))
    return np.concatenate([r["out"] for r in res.results], axis=0)
